# version 06
# baseline (speedup 1.0000x reference)
import numpy as np
import ml_dtypes

import concourse.bass as bass
import concourse.mybir as mybir
from concourse.bass_utils import run_bass_kernel_spmd

F32 = mybir.dt.float32
BF16 = mybir.dt.bfloat16
I32 = mybir.dt.int32
AF = mybir.ActivationFunctionType
ALU = mybir.AluOpType
AX = mybir.AxisListType

D = 1024
NH = 16
DH = 64
FF = 4096
EPS = 1e-5
N_CORES = 8
RB = 4
EB = 8

BF16_NP = ml_dtypes.bfloat16


class Prog:

    ENGS = ("sp", "pl", "aq", "pe", "act", "dve")
    QUEUES = ("sp", "pl", "aq")
    W = 8

    def __init__(self, nc):
        self.nc = nc
        self.sems = {e: nc.alloc_semaphore(f"s_{e}") for e in ("pe", "act", "dve")}
        self.qsems = {q: [nc.alloc_semaphore(f"s_{q}{i}") for i in range(self.W)]
                      for q in self.QUEUES}
        self.steps = {e: [] for e in self.ENGS}
        self.cum = {e: 0 for e in ("pe", "act", "dve")}
        self.qcum = {q: [0] * self.W for q in self.QUEUES}
        self.qidx = {q: 0 for q in self.QUEUES}
        self.res = {}
        self.small_sids = set()

    def step(self, eng, emit, reads=(), writes=(), name="", small=False):
        deps = {}

        def _add(sid):
            if sid is None:
                return
            if sid[0] == "c":
                if sid[1] != eng or sid in self.small_sids:
                    key = ("c", sid[1])
                    deps[key] = max(deps.get(key, 0), sid[2])
            else:
                if sid[1] != eng:
                    key = ("q", sid[1], sid[2])
                    deps[key] = max(deps.get(key, 0), sid[3])

        for r in reads:
            ent = self.res.get(r)
            if ent:
                _add(ent[0])
        for r in writes:
            ent = self.res.get(r)
            if ent:
                _add(ent[0])
                for rd in ent[1]:
                    _add(rd)
        self.cum[eng] += 1
        sid = ("c", eng, self.cum[eng])
        if small:
            self.small_sids.add(sid)
        self.steps[eng].append(("c", emit, deps, name, None))
        self._commit(sid, reads, writes)
        return sid

    def dma(self, q, outs_ins, reads=(), writes=(), name="", transpose=False):
        deps = {}

        def _add(sid):
            if sid is None:
                return
            if sid[0] == "c":
                key = ("c", sid[1])
                deps[key] = max(deps.get(key, 0), sid[2])
            else:
                key = ("q", sid[1], sid[2])
                deps[key] = max(deps.get(key, 0), sid[3])

        for r in reads:
            ent = self.res.get(r)
            if ent:
                _add(ent[0])
        for r in writes:
            ent = self.res.get(r)
            if ent:
                _add(ent[0])
                for rd in ent[1]:
                    _add(rd)
        slot = self.qidx[q] % self.W
        self.qidx[q] += 1
        before = self.qcum[q][slot]
        self.qcum[q][slot] += 16 * len(outs_ins)
        sid = ("q", q, slot, self.qcum[q][slot])
        sem = self.qsems[q][slot]

        def emit(obj, outs_ins=outs_ins, sem=sem, transpose=transpose):
            for o, i in outs_ins:
                obj.dma_start(o, i, transpose=transpose).then_inc(sem, 16)

        host = "act" if q == "aq" else q
        self.steps[host].append(("q", emit, deps, name, (q, slot, before)))
        self._commit(sid, reads, writes)
        return sid

    def _commit(self, sid, reads, writes):
        for r in writes:
            self.res[r] = [sid, []]
        for r in reads:
            ent = self.res.setdefault(r, [None, []])
            ent[1].append(sid)

    def emit_engine(self, eng, obj):
        waited = {}
        for kind, emit, deps, name, qinfo in self.steps[eng]:
            if qinfo is not None:
                qq, slot, before = qinfo
                if before > 0:
                    obj.wait_ge(self.qsems[qq][slot], before)
            for key, dv in sorted(deps.items(), key=str):
                if dv > waited.get(key, 0):
                    sem = (self.sems[key[1]] if key[0] == "c"
                           else self.qsems[key[1]][key[2]])
                    obj.wait_ge(sem, dv)
                    waited[key] = dv
            last = emit(obj)
            if eng in ("pe", "act", "dve") and last is not None:
                last.then_inc(self.sems[eng], 1)

    def queue_fin(self, q):
        return [(self.qsems[q][s], self.qcum[q][s]) for s in range(self.W)
                if self.qcum[q][s] > 0]


def _log_pos_const(L):
    lp = np.zeros((2, L), np.float32)
    lp[0] = np.log(np.arange(L, dtype=np.float64) + 1.0).astype(np.float32)
    lp[1] = 1.0
    return lp


def build_program(layers=(0, 1), n_batch=2, L=4096, detect_races=False, dbg=False, l0_lvl=99, ph0=True, setup=True):
    NCH = L // 512
    n_tiles = n_batch * NCH
    nc = bass.Bass("TRN2", target_bir_lowering=False,
                   detect_race_conditions=detect_races)

    x_in = nc.dram_tensor("x_in", [n_batch * L, D], F32, kind="ExternalInput")
    x_out = nc.dram_tensor("x_out", [n_batch * L, D], F32, kind="ExternalOutput")

    win = {}
    def WIN(name, shape, dtype=F32):
        win[name] = nc.dram_tensor(name, shape, dtype, kind="ExternalInput")

    if 0 in layers:
        WIN("L0_attn_Wq", [D, D]); WIN("L0_attn_Wv", [D, D])
        WIN("L0_attn_Wg", [D, D]); WIN("L0_attn_Wo", [D, D])
        WIN("L0_attn_beta", [NH]); WIN("L0_attn_ln_gamma", [D]); WIN("L0_attn_ln_beta", [D])
        WIN("L0_ffn_w1", [D, FF]); WIN("L0_ffn_b1", [FF])
        WIN("L0_ffn_w2", [FF, D]); WIN("L0_ffn_b2", [D])
    if 1 in layers:
        WIN("L1_attn_Wqkv", [3 * D, D]); WIN("L1_attn_Wout", [D, D]); WIN("L1_attn_bout", [D])
        WIN("L1_attn_Wl1", [64, 1]); WIN("L1_attn_bl1", [64])
        WIN("L1_attn_Wl2", [NH, 64]); WIN("L1_attn_bl2", [NH])
        WIN("L1_ffn_ln_gamma", [D]); WIN("L1_ffn_ln_beta", [D])
        WIN("L1_ffn_fc1_q", [D, FF], I32); WIN("L1_ffn_fc1_scales", [8, FF])
        WIN("L1_ffn_fc1_bias", [FF])
        WIN("L1_ffn_fc2_q", [FF, D], I32); WIN("L1_ffn_fc2_scales", [32, D])
        WIN("L1_ffn_fc2_bias", [D])

    dbg_out = {}
    def DBG(name, buf_ap, res, cols=4096, dt=BF16):
        if not dbg:
            return
        t = nc.dram_tensor(f"dbg_{name}", [128, cols], dt, kind="ExternalOutput")
        dbg_out[name] = t
        pl_dma_ref[0]([(t.ap(), buf_ap)], reads=[res], writes=[f"dbgd_{name}"], name=f"dbg{name}")

    pl_dma_ref = [None]
    dcache = {}
    def DC(name, shape, dtype=BF16):
        dcache[name] = nc.dram_tensor(name, shape, dtype, kind="Internal")

    if 0 in layers:
        DC("d_wq", [D, D]); DC("d_wv", [D, D]); DC("d_wg", [D, D]); DC("d_wo", [D, D])
        DC("d_wqT", [D, D]); DC("d_wvT", [D, D]); DC("d_wgT", [D, D]); DC("d_woT", [D, D])
        DC("d_w1", [D, FF]); DC("d_w2", [FF, D])
    if 1 in layers:
        DC("d_wqkv", [3 * D, D]); DC("d_wout", [D, D])
        DC("d_wqkvT", [D, 3 * D]); DC("d_woutT", [D, D])
        DC("d_w1d", [D, FF]); DC("d_w2d", [FF, D])
        DC("d_dn", [NH, 512], BF16)

    eye_bf_d = nc.inline_tensor(np.eye(128).astype(BF16_NP), "c_eye_bf")
    eye_f_d = nc.inline_tensor(np.eye(128, dtype=np.float32), "c_eye_f")
    if 0 in layers:
        jj, ii = np.mgrid[0:128, 0:128]
        delta_d = nc.inline_tensor((ii - jj).astype(np.float32), "c_delta")
        tri01_d = nc.inline_tensor((ii >= jj).astype(np.float32), "c_tri01")
        trineg_d = nc.inline_tensor(
            np.where(ii >= jj, 0.0, -30000.0).astype(np.float32), "c_trineg")
    if 1 in layers:
        lp_d = nc.inline_tensor(_log_pos_const(L), "c_logpos")
        e16 = np.zeros((NH, 128, NH), np.float32)
        for h in range(NH):
            e16[h, :, h] = 1.0
        e16_d = nc.inline_tensor(e16.reshape(NH * 128, NH).astype(BF16_NP), "c_e16")

    SB = nc.alloc_sbuf_tensor
    eye_bf = SB("eye_bf", [128, 128], BF16)
    eye_f = SB("eye_f", [128, 128], F32)
    ones_bf = SB("ones_bf", [1, 128], BF16)
    ones_f = SB("ones_f", [1, 128], F32)
    cst_eps = SB("cst_eps", [128, 1], F32)
    stats = SB("stats", [128, 64], F32)
    stats_b = SB("stats_b", [128, 64], F32)
    x_t = SB("x_t", [128, 4 * D], F32)
    x1_t = SB("x1_t", [128, 4 * D], F32)
    scr_f = SB("scr_f", [128, 4 * D], F32)
    xn_tmp = SB("xn_tmp", [128, 4 * D], BF16)
    xn_fm = SB("xn_fm", [128, 8 * 512], BF16)
    h_fm = SB("h_fm", [128, 32 * 512], BF16)
    g_t = SB("g_t", [128, 4 * D], BF16)
    q_t = SB("q_t", [128, 4 * D], BF16)
    v_t = SB("v_t", [128, 4 * D], BF16)
    y_t = SB("y_t", [128, 4 * D], BF16)
    g0_t = SB("g0_t", [128, 4 * D], BF16)
    yg_fm = SB("yg_fm", [128, 8 * 512], BF16)
    xn1_fm = SB("xn1_fm", [128, 8 * 512], BF16)
    ln3_fm = None
    ln3_fm = xn1_fm
    wbuf = [SB(f"wbuf{i}", [128, 8 * 512], BF16) for i in range(2)]
    w1sl = [SB(f"w1sl{i}", [128, 2048], BF16) for i in range(2)]
    w2sl = [SB(f"w2sl{i}", [128, 1024], BF16) for i in range(2)]
    b2_bc = SB("b2_bc", [128, D], BF16)
    bout_bc = SB("bout_bc", [128, D], BF16)
    g0g_bc = SB("g0g_bc", [128, D], BF16)
    b0b_bc = SB("b0b_bc", [128, D], BF16)
    hfc2_bc = SB("hfc2_bc", [128, D], BF16)
    b1_col = SB("b1_col", [128, 32], F32)
    fc1b_col = SB("fc1b_col", [128, 32], F32)
    g1_col = SB("g1_col", [128, 8], F32)
    bt1_col = SB("bt1_col", [128, 8], F32)
    lt_all = SB("lt_all", [128, NH * 128], BF16)
    lam_row = SB("lam_row", [1, NH], F32)
    lnlam_cols = SB("lnlam_cols", [128, NH], F32)
    carry_sb = SB("carry_sb", [1, D], BF16)
    lamT = SB("lamT", [128, (L // 128) * NH], F32)
    wl1b = SB("wl1b", [2, 64], F32)
    wl2t = SB("wl2t", [64, 16], F32)
    wl2_sb = SB("wl2_sb", [16, 64], F32)
    bl2c = SB("bl2c", [16, 1], F32)
    e16_sb = SB("e16_sb", [128, 16 * NH], BF16)
    dn_sb = SB("dn_sb", [16, 512], BF16)
    rd_row = [SB(f"rd_row{i}", [1, 512], BF16) for i in range(2)]
    scrow = [SB(f"scrow{i}", [1, 512], BF16) for i in range(2)]
    misc_r = SB("misc_r", [1, 64], F32)
    delta_sb = x1_t[:, 0:128]
    tri01_sb = x1_t[:, 128:256]
    trineg_sb = x1_t[:, 256:384]

    PS = nc.alloc_psum_tensor
    pb = [PS(f"pb{i}", [128, 512], F32) for i in range(7)]
    pbt = PS("pbt", [128, 1024], BF16)
    carry_ps = pbt[0:1, :].bitcast(F32)
    pbt_f32 = pbt[:, :].bitcast(F32)

    def bank_ap(g):
        return pb[g][:] if g < 7 else pbt_f32

    def bank_res(g):
        return f"pb{g}" if g < 7 else "pbt"

    with nc.Block() as blk0:
        @blk0.gpsimd
        def _(g):
            g.memset(cst_eps[:], EPS)
            g.memset(ones_bf[:], 1.0)
            g.memset(ones_f[:], 1.0)
            g.memset(carry_sb[:], 0.0)
    nc.const_aps.register(F32, EPS, cst_eps[:])

    P = Prog(nc)
    S = P.step

    def sp_dma(outs_ins, reads, writes, name=""):
        return P.dma("sp", outs_ins, reads, writes, name=name)

    XBAR_Q = "aq"

    def sp_dma_t(outs_ins, reads, writes, name=""):
        return P.dma(XBAR_Q, outs_ins, reads, writes, name=name, transpose=True)

    def pl_dma(outs_ins, reads, writes, name=""):
        return P.dma("pl", outs_ins, reads, writes, name=name)
    pl_dma_ref[0] = pl_dma

    def cast_weight(src, dst, R, C, tag):
        for si, r0 in enumerate(range(0, R, 128)):
            slot = si % 4
            buf = h_fm[:, slot * 4096:slot * 4096 + C]
            pl_dma([(buf, src.ap()[r0:r0 + 128, :])],
                   reads=[f"cstg{slot}"], writes=[f"cstg{slot}"], name=f"c_{tag}")
            sp_dma([(dst.ap()[r0:r0 + 128, :], buf)],
                   reads=[f"cstg{slot}"], writes=[f"d_{tag}"])

    def build_wt(bname, tname, FT):
        for fc in range(FT // 512):
            for eb in range(EB):
                stg = wbuf[0][:, 512 * (eb % 4):512 * (eb % 4) + 512]
                P.dma("aq", [(stg, dcache[bname].ap()[512 * fc:512 * fc + 512,
                                                      128 * eb:128 * eb + 128])],
                      reads=[f"d_{bname[2:]}", f"wt{eb % 4}"], writes=[f"wt{eb % 4}"],
                      name="wtx", transpose=True)
                sp_dma([(dcache[tname].ap()[128 * eb:128 * eb + 128,
                                            512 * fc:512 * fc + 512], stg)],
                       reads=[f"wt{eb % 4}"], writes=[f"d_{tname[2:]}"], name="wts")

    if 0 in layers and ph0:
        cast_weight(win["L0_ffn_w1"], dcache["d_w1"], D, FF, "w1")
        cast_weight(win["L0_ffn_w2"], dcache["d_w2"], FF, D, "w2")
        cast_weight(win["L0_attn_Wq"], dcache["d_wq"], D, D, "wq")
        cast_weight(win["L0_attn_Wv"], dcache["d_wv"], D, D, "wv")
        cast_weight(win["L0_attn_Wg"], dcache["d_wg"], D, D, "wg")
        build_wt("d_wq", "d_wqT", D)
        build_wt("d_wv", "d_wvT", D)
        build_wt("d_wg", "d_wgT", D)
        cast_weight(win["L0_attn_Wo"], dcache["d_wo"], D, D, "wo")
        build_wt("d_wo", "d_woT", D)
    if 1 in layers and ph0:
        cast_weight(win["L1_attn_Wqkv"], dcache["d_wqkv"], 3 * D, D, "wqkv")
        build_wt("d_wqkv", "d_wqkvT", 3 * D)
        cast_weight(win["L1_attn_Wout"], dcache["d_wout"], D, D, "wout")
        build_wt("d_wout", "d_woutT", D)

        def dequant(qsrc, ssrc, dst, R, C, tag):
            for si, r0 in enumerate(range(0, R, 128)):
                qb = x_t[:].bitcast(I32)[:, 0:C]
                pl_dma([(qb, qsrc.ap()[r0:r0 + 128, :])],
                       reads=["x"], writes=["x"], name=f"dq_{tag}")
                S("dve", lambda d, qb=qb, C=C: d.tensor_copy(scr_f[:, 0:C], qb),
                  reads=["x"], writes=["scr"])
                for ci, c0 in enumerate(range(0, C, 512)):
                    sr = scrow[ci % 2]
                    pl_dma([(sr[:], ssrc.ap()[si:si + 1, c0:c0 + 512])],
                           reads=[f"sr{ci % 2}"], writes=[f"sr{ci % 2}"])
                    bk = ci % 2
                    S("pe", lambda t, sr=sr, bk=bk: t.matmul(
                        pb[bk][:], ones_bf[:], sr[:], start=True, stop=True),
                      reads=[f"sr{ci % 2}"], writes=[f"pb{bk}"])
                    S("dve", lambda d, c0=c0, bk=bk: d.scalar_tensor_tensor(
                        g0_t[:, c0:c0 + 512], scr_f[:, c0:c0 + 512], -128.0,
                        pb[bk][:], ALU.add, ALU.mult),
                      reads=["scr", f"pb{bk}"], writes=["g0", f"pb{bk}"])
                sp_dma([(dst.ap()[r0:r0 + 128, :], g0_t[:, 0:C])],
                       reads=["g0"], writes=[f"d_{tag}"])

        dequant(win["L1_ffn_fc1_q"], win["L1_ffn_fc1_scales"], dcache["d_w1d"], D, FF, "w1d")
        dequant(win["L1_ffn_fc2_q"], win["L1_ffn_fc2_scales"], dcache["d_w2d"], FF, D, "w2d")

    S("dve", lambda d: d.nop(),
      reads=["wt0", "wt1", "wt2", "wt3"], writes=["wbuf0", "wbuf1"], name="wtfence")

    sp_dma([(eye_bf[:], eye_bf_d.ap()), (eye_f[:], eye_f_d.ap())],
           reads=[], writes=["eye"])

    def bc_row(src_ap_row, dst, dst_res, scale=1.0):
        for c0 in range(0, D, 512):
            row = scrow[0]
            pl_dma([(row[:], src_ap_row[0:1, c0:c0 + 512])],
                   reads=["sr0"], writes=["sr0"], name="bcr")
            S("pe", lambda t, row=row: t.matmul(pb[2][:], ones_bf[:], row[:],
                                                start=True, stop=True),
              reads=["sr0"], writes=["pb2"])
            S("act", lambda a, c0=c0, dst=dst, scale=scale: a.activation(
                dst[:, c0:c0 + 512], pb[2][:], AF.Copy, scale=scale),
              reads=["pb2"], writes=[dst_res, "pb2"])

    def col_build(src_ap, dst, n, dst_res):
        sp_dma([(scr_f[0:n, 0:128], src_ap.rearrange("(a p) -> a p", p=128))],
               reads=["scr"], writes=["scr"], name="colb")
        S("pe", lambda t, n=n: t.transpose(pb[3][:, 0:n], scr_f[0:n, 0:128],
                                           eye_f[0:n, 0:n]),
          reads=["scr", "eye"], writes=["pb3"])
        S("act", lambda a, n=n, dst=dst: a.activation(dst[:, 0:n], pb[3][:, 0:n], AF.Copy),
          reads=["pb3"], writes=[dst_res, "pb3"])

    def row1(ap):
        return ap.rearrange("(a b) -> a b", a=1)

    if 0 in layers and setup:
        sp_dma([(delta_sb, delta_d.ap()), (tri01_sb, tri01_d.ap()),
                (trineg_sb, trineg_d.ap()),
                (misc_r[0:1, 0:16], row1(win["L0_attn_beta"].ap()))],
               reads=["x1"], writes=["scanc", "x1"])
        S("act", lambda a: (a.activation(lam_row[:], misc_r[0:1, 0:16], AF.Sigmoid),
                            a.nop())[0],
          reads=["scanc"], writes=["lamr"], small=True)
        S("act", lambda a: (a.activation(misc_r[0:1, 16:32], lam_row[:], AF.Ln),
                            a.nop())[0],
          reads=["lamr", "scanc"], writes=["scanc"], small=True)
        for h in range(NH):
            S("pe", lambda t, h=h: t.matmul(pb[0][:, h:h + 1], ones_f[:],
                                            misc_r[0:1, 16 + h:17 + h],
                                            start=True, stop=True),
              reads=["scanc"], writes=["pb0"], small=True)
        S("act", lambda a: a.activation(lnlam_cols[:], pb[0][:, 0:NH], AF.Copy),
          reads=["pb0"], writes=["lnlamc", "pb0"], small=True)
        for h in range(NH):
            S("act", lambda a, h=h: (a.activation(scr_f[:, 0:128], delta_sb,
                                                  AF.Identity,
                                                  scale=lnlam_cols[:, h:h + 1]),
                                     a.nop())[0],
              reads=["lnlamc", "scanc", "scr", "x1"], writes=["scr"], small=True)
            S("dve", lambda d: (d.tensor_mul(scr_f[:, 128:256], scr_f[:, 0:128],
                                             tri01_sb), d.nop())[0],
              reads=["scr", "scanc", "x1"], writes=["scr"], small=True)
            S("dve", lambda d: (d.tensor_add(scr_f[:, 128:256], scr_f[:, 128:256],
                                             trineg_sb), d.nop())[0],
              reads=["scr", "scanc", "x1"], writes=["scr"], small=True)
            S("act", lambda a, h=h: a.activation(lt_all[:, 128 * h:128 * h + 128],
                                                 scr_f[:, 128:256], AF.Exp),
              reads=["scr"], writes=["lt"], small=True)
        bc_row(row1(win["L0_attn_ln_gamma"].ap()), g0g_bc, "g0g")
        bc_row(row1(win["L0_attn_ln_beta"].ap()), b0b_bc, "b0b")
        bc_row(row1(win["L0_ffn_b2"].ap()), b2_bc, "b2bc")
        col_build(win["L0_ffn_b1"].ap(), b1_col, 32, "b1c")

    if 1 in layers and setup:
        bc_row(row1(win["L1_attn_bout"].ap()), bout_bc, "boutb")
        bc_row(row1(win["L1_ffn_fc2_bias"].ap()), hfc2_bc, "hfc2", scale=0.5)
        col_build(win["L1_ffn_fc1_bias"].ap(), fc1b_col, 32, "fc1c")
        col_build(win["L1_ffn_ln_gamma"].ap(), g1_col, 8, "g1c")
        col_build(win["L1_ffn_ln_beta"].ap(), bt1_col, 8, "bt1c")
        sp_dma([(e16_sb[:].rearrange("p (a f) -> p a f", f=NH),
                 e16_d.ap().rearrange("(a p) f -> p a f", p=128))],
               reads=[], writes=["e16"])
        lp_sb = x_t
        sp_dma([(wl1b[0:1, :], win["L1_attn_Wl1"].ap().rearrange("a b -> b a")),
                (wl1b[1:2, :], row1(win["L1_attn_bl1"].ap())),
                (bl2c[:], win["L1_attn_Wl2"].ap()[0:0 + NH, 0:1] if False else
                 win["L1_attn_bl2"].ap().rearrange("(p a) -> p a", a=1)),
                (wl2_sb[:], win["L1_attn_Wl2"].ap())],
               reads=[], writes=["lamnc"])
        S("pe", lambda t: t.transpose(pb[4][0:64, 0:16], wl2_sb[:], eye_f[0:16, 0:16]),
          reads=["lamnc", "eye"], writes=["pb4"])
        S("act", lambda a: a.activation(wl2t[:, 0:16], pb[4][0:64, 0:16], AF.Copy),
          reads=["pb4"], writes=["wl2t", "pb4"], small=True)
        for half in range((L + 2047) // 2048):
            c0 = half * 2048
            cw = min(2048, L - c0)
            sp_dma([(lp_sb[0:2, 0:cw], lp_d.ap()[0:2, c0:c0 + cw])],
                   reads=["x"], writes=["x"])
            for j in range(cw // 128):
                bk = 4 + (j % 2)
                S("pe", lambda t, j=j, bk=bk: t.matmul(
                    pb[bk][0:64, 0:128], wl1b[:], lp_sb[0:2, 128 * j:128 * j + 128],
                    start=True, stop=True),
                  reads=["lamnc", "x"], writes=[f"pb{bk}"])
                S("act", lambda a, j=j, c0=c0, bk=bk: a.activation(
                    scr_f[0:64, c0 + 128 * j:c0 + 128 * j + 128],
                    pb[bk][0:64, 0:128], AF.Relu),
                  reads=[f"pb{bk}"], writes=["scr", f"pb{bk}"])
        for j in range(L // 512):
            S("pe", lambda t, j=j: t.matmul(pb[5][0:16, :], wl2t[:, 0:16],
                                            scr_f[0:64, 512 * j:512 * j + 512],
                                            start=True, stop=True),
              reads=["wl2t", "scr"], writes=["pb5"])
            S("act", lambda a, j=j: a.activation(x1_t[0:16, 512 * j:512 * j + 512],
                                                 pb[5][0:16, :], AF.Sigmoid,
                                                 bias=bl2c[0:16, 0:1]),
              reads=["pb5", "lamnc"], writes=["x1", "pb5"])
        for kb in range(L // 128):
            S("pe", lambda t, kb=kb: t.transpose(pb[6][:, 0:16],
                                                 x1_t[0:16, 128 * kb:128 * kb + 128],
                                                 eye_f[0:16, 0:16]),
              reads=["x1", "eye"], writes=["pb6"])
            S("act", lambda a, kb=kb: a.activation(lamT[:, 16 * kb:16 * kb + 16],
                                                   pb[6][:, 0:16], AF.Copy),
              reads=["pb6"], writes=["lamT", "pb6"])

    def ln_stats(src_tile, src_res, c, ST, STR, sq, sq_res):
        def _n(d, ins):
            d.nop()
            return ins
        for rb in range(RB):
            S("dve", lambda d, rb=rb: d.tensor_mul(
                sq[:, 1024 * rb:1024 * rb + 1024],
                src_tile[:, 1024 * rb:1024 * rb + 1024],
                src_tile[:, 1024 * rb:1024 * rb + 1024]),
              reads=[src_res, sq_res], writes=[sq_res])
        for rb in range(RB):
            S("dve", lambda d, rb=rb: _n(d, d.reduce_sum(
                ST[:, c + 16 + rb:c + 17 + rb],
                src_tile[:, 1024 * rb:1024 * rb + 1024], axis=AX.X)),
              reads=[src_res], writes=[STR], small=True)
            S("dve", lambda d, rb=rb: _n(d, d.reduce_sum(
                ST[:, c + 20 + rb:c + 21 + rb],
                sq[:, 1024 * rb:1024 * rb + 1024], axis=AX.X)),
              reads=[sq_res], writes=[STR], small=True)
        S("dve", lambda d: _n(d, d.tensor_scalar_mul(ST[:, c:c + 4],
                                                     ST[:, c + 16:c + 20], 1.0 / D)),
          reads=[STR], writes=[STR], small=True)
        S("dve", lambda d: _n(d, d.tensor_scalar_mul(ST[:, c + 12:c + 16],
                                                     ST[:, c + 20:c + 24], 1.0 / D)),
          reads=[STR], writes=[STR], small=True)
        S("dve", lambda d: _n(d, d.tensor_mul(ST[:, c + 24:c + 28], ST[:, c:c + 4],
                                              ST[:, c:c + 4])),
          reads=[STR], writes=[STR], small=True)
        S("dve", lambda d: _n(d, d.tensor_sub(ST[:, c + 20:c + 24],
                                              ST[:, c + 12:c + 16],
                                              ST[:, c + 24:c + 28])),
          reads=[STR], writes=[STR], small=True)
        S("act", lambda a: _n(a, a.activation(ST[:, c + 24:c + 28],
                                              ST[:, c + 20:c + 24],
                                              AF.Sqrt, bias=EPS)),
          reads=[STR], writes=[STR], small=True)
        S("dve", lambda d: _n(d, d.reciprocal(ST[:, c + 4:c + 8],
                                              ST[:, c + 24:c + 28])),
          reads=[STR], writes=[STR], small=True)
        S("dve", lambda d: _n(d, d.tensor_mul(ST[:, c + 8:c + 12], ST[:, c:c + 4],
                                              ST[:, c + 4:c + 8])),
          reads=[STR], writes=[STR], small=True)
        S("dve", lambda d: _n(d, d.tensor_scalar_mul(ST[:, c + 8:c + 12],
                                                     ST[:, c + 8:c + 12], -1.0)),
          reads=[STR], writes=[STR], small=True)

    h_f32 = h_fm[:].bitcast(F32)

    def stats_of(t):
        return (stats if t % 2 == 0 else stats_b), f"st{t % 2}"

    def l0_prolog(t):
        ST, STR = stats_of(t)
        ln_stats(x_t, "x", 0, ST, STR, h_f32[:, 0:4096], "h")
        for rb in range(RB):
            S("act", lambda a, rb=rb, ST=ST: a.activation(
                xn_tmp[:, 1024 * rb:1024 * rb + 1024],
                x_t[:, 1024 * rb:1024 * rb + 1024],
                AF.Identity, scale=ST[:, 4 + rb:5 + rb],
                bias=ST[:, 8 + rb:9 + rb]),
              reads=["x", STR, "xn_tmp"], writes=["xn_tmp"], name="ln0ap")

    def transpose_tile(src_bf, dst_fm, src_res, dst_res):
        for rb in range(RB):
            S("pe", lambda t, rb=rb: [
                t.transpose(pbt[:, 128 * eb:128 * eb + 128],
                            src_bf[:, 1024 * rb + 128 * eb:1024 * rb + 128 * eb + 128],
                            eye_bf[:]) for eb in range(EB)][-1],
              reads=[src_res, "eye", "pbt"], writes=["pbt"])
            S("dve", lambda d, rb=rb: [
                d.tensor_copy(dst_fm[:, 512 * eb + 128 * rb:512 * eb + 128 * rb + 128],
                              pbt[:, 128 * eb:128 * eb + 128]) for eb in range(EB)][-1],
              reads=["pbt"], writes=[dst_res])

    def stream_wt(dname, fc, tag=""):
        buf = wbuf[fc % 2]
        tsrc = dcache[dname + "T"]
        sp_dma([(buf[:].rearrange("p (a f) -> p a f", f=512),
                 tsrc.ap()[:, 512 * fc:512 * fc + 512]
                 .rearrange("(a p) f -> p a f", p=128))],
               reads=[f"d_{dname[2:]}T", f"wbuf{fc % 2}"],
               writes=[f"wbuf{fc % 2}"], name=tag)
        return buf

    def proj_tm(dname, src_fm, src_res, dst, dst_res, act_fn, act_kw=None, tag=""):
        for fc in range(2):
            buf = stream_wt(dname, fc, tag=tag)
            for tb in range(RB):
                bk = tb % 2
                S("pe", lambda t, buf=buf, tb=tb, bk=bk: [
                    t.matmul(pb[bk][:],
                             src_fm[:, 512 * eb + 128 * tb:512 * eb + 128 * tb + 128],
                             buf[:, 512 * eb:512 * eb + 512],
                             start=(eb == 0), stop=(eb == EB - 1)) for eb in range(EB)][-1],
                  reads=[src_res, f"wbuf{fc % 2}"], writes=[f"pb{bk}"], name=f"{tag}mm")
                kw = dict(act_kw or {})
                S("act", lambda a, dst=dst, tb=tb, fc=fc, bk=bk, kw=kw: a.activation(
                    dst[:, 1024 * tb + 512 * fc:1024 * tb + 512 * fc + 512],
                    pb[bk][:], act_fn, **kw),
                  reads=[f"pb{bk}"] + list((act_kw or {}).pop("_res", [])),
                  writes=[dst_res, f"pb{bk}"], name=f"{tag}ev")

    for b in range(n_batch):
        for ch in range(NCH):
            t_idx = b * NCH + ch
            row0 = b * L + ch * 512
            ST, STR = stats_of(t_idx)
            if t_idx == 0:
                sp_dma([(x_t[:].rearrange("p (a f) -> p a f", f=D),
                         x_in.ap()[row0:row0 + 512, :].rearrange("(a p) f -> p a f", p=128))],
                       reads=["x"], writes=["x"], name="xload")
                if 0 in layers and l0_lvl >= 1:
                    l0_prolog(0)

            if 0 in layers and l0_lvl >= 1:
                transpose_tile(xn_tmp, xn_fm, "xn_tmp", "xn_fm")

                if l0_lvl >= 4:
                    proj_tm("d_wq", xn_fm, "xn_fm", q_t, "q", AF.Copy, tag="q")
                    proj_tm("d_wv", xn_fm, "xn_fm", v_t, "v", AF.Copy, tag="v")
                    proj_tm("d_wg", xn_fm, "xn_fm", g0_t, "g0", AF.Silu, tag="gp")

                if t_idx == 0:
                    DBG("q", q_t[:], "q")
                    DBG("v", v_t[:], "v")
                    DBG("g0", g0_t[:], "g0")
                if ch == 0 and l0_lvl >= 5:
                    S("dve", lambda d: d.memset(carry_sb[:], 0.0),
                      reads=["carrysb"], writes=["carrysb"])
                for half in range(2 if l0_lvl >= 5 else 0):
                    hs = range(8 * half, 8 * half + 8)
                    for rc in range(RB):
                        if rc == 0:
                            S("dve", lambda d, half=half: [
                                d.scalar_tensor_tensor(
                                    v_t[0:1, 64 * h:64 * h + 64],
                                    carry_sb[0:1, 64 * h:64 * h + 64],
                                    lam_row[0:1, h:h + 1],
                                    v_t[0:1, 64 * h:64 * h + 64],
                                    ALU.mult, ALU.add)
                                for h in range(8 * half, 8 * half + 8)][-1],
                              reads=["carrysb", "lamr", "v"], writes=["v"], small=True)
                        else:
                            S("dve", lambda d, half=half, rc=rc: [
                                d.scalar_tensor_tensor(
                                    v_t[0:1, 1024 * rc + 64 * h:1024 * rc + 64 * h + 64],
                                    carry_ps[0:1, 64 * (h % 8):64 * (h % 8) + 64],
                                    lam_row[0:1, h:h + 1],
                                    v_t[0:1, 1024 * rc + 64 * h:1024 * rc + 64 * h + 64],
                                    ALU.mult, ALU.add)
                                for h in range(8 * half, 8 * half + 8)][-1],
                              reads=["pbt", "lamr", "v"], writes=["v"], small=True)
                        sbk = 4 if rc % 2 == 0 else 0
                        S("pe", lambda t, half=half, rc=rc, sbk=sbk: [
                            ins for h in range(8 * half, 8 * half + 8)
                            for ins in (
                                t.matmul(pb[sbk][:, 64 * (h % 8):64 * (h % 8) + 64],
                                         lt_all[:, 128 * h:128 * h + 128],
                                         v_t[:, 1024 * rc + 64 * h:1024 * rc + 64 * h + 64],
                                         start=True, stop=True),
                                t.matmul(carry_ps[0:1, 64 * (h % 8):64 * (h % 8) + 64],
                                         lt_all[:, 128 * h + 127:128 * h + 128],
                                         v_t[:, 1024 * rc + 64 * h:1024 * rc + 64 * h + 64],
                                         start=True, stop=True))][-1],
                          reads=["v", "lt", f"pb{sbk}", "pbt"], writes=[f"pb{sbk}", "pbt"],
                          name="scan")
                        S("dve", lambda d, half=half, rc=rc, sbk=sbk: d.tensor_mul(
                            y_t[:, 1024 * rc + 512 * half:1024 * rc + 512 * half + 512],
                            q_t[:, 1024 * rc + 512 * half:1024 * rc + 512 * half + 512],
                            pb[sbk][:]),
                          reads=[f"pb{sbk}", "q"], writes=["y", f"pb{sbk}"])
                        if rc == RB - 1:
                            S("act", lambda a, half=half: a.activation(
                                carry_sb[0:1, 512 * half:512 * half + 512],
                                carry_ps[0:1, 0:512], AF.Copy),
                              reads=["pbt"], writes=["carrysb", "pbt"], small=True)

                if t_idx == 0:
                    DBG("y", y_t[:], "y")
                if l0_lvl < 7:
                    for rb in range(RB):
                        S("dve", lambda d, rb=rb: d.tensor_copy(
                            x1_t[:, 1024 * rb:1024 * rb + 1024],
                            x_t[:, 1024 * rb:1024 * rb + 1024]),
                          reads=["x"], writes=["x1"])
                if l0_lvl >= 6:
                    ln_stats(y_t, "y", 32, ST, STR, scr_f, "scr")
                for rb in range(RB if l0_lvl >= 6 else 0):
                    S("act", lambda a, rb=rb, ST=ST: a.activation(
                        xn_tmp[:, 1024 * rb:1024 * rb + 1024],
                        y_t[:, 1024 * rb:1024 * rb + 1024],
                        AF.Identity, scale=ST[:, 36 + rb:37 + rb],
                        bias=ST[:, 40 + rb:41 + rb]),
                      reads=["y", STR, "xn_tmp"], writes=["xn_tmp"], name="ln1ap")
                for rb in range(RB if l0_lvl >= 6 else 0):
                    S("dve", lambda d, rb=rb: d.tensor_mul(
                        scr_f[:, 1024 * rb:1024 * rb + 1024],
                        xn_tmp[:, 1024 * rb:1024 * rb + 1024], g0g_bc[:]),
                      reads=["xn_tmp", "g0g", "scr"], writes=["scr"])
                    S("dve", lambda d, rb=rb: d.tensor_add(
                        scr_f[:, 1024 * rb:1024 * rb + 1024],
                        scr_f[:, 1024 * rb:1024 * rb + 1024], b0b_bc[:]),
                      reads=["scr", "b0b"], writes=["scr"])
                    S("dve", lambda d, rb=rb: d.tensor_mul(
                        y_t[:, 1024 * rb:1024 * rb + 1024],
                        scr_f[:, 1024 * rb:1024 * rb + 1024],
                        g0_t[:, 1024 * rb:1024 * rb + 1024]),
                      reads=["scr", "g0"], writes=["y"])
                for j2 in range(16 if l0_lvl >= 2 else 0):
                    buf = w1sl[j2 % 2]
                    sp_dma([(buf[:].rearrange("p (a f) -> p a f", f=256),
                             dcache["d_w1"].ap()[:, 256 * j2:256 * j2 + 256]
                             .rearrange("(a p) f -> p a f", p=128))],
                           reads=["d_w1", f"w1sl{j2 % 2}"], writes=[f"w1sl{j2 % 2}"],
                           name="w1sl")
                    for k in range(2):
                        hb = 2 * j2 + k
                        bk = hb % 2
                        S("pe", lambda t, buf=buf, k=k, bk=bk: [
                            t.matmul(pb[bk][:],
                                     buf[:, 256 * eb + 128 * k:256 * eb + 128 * k + 128],
                                     xn_fm[:, 512 * eb:512 * eb + 512],
                                     start=(eb == 0), stop=(eb == EB - 1))
                            for eb in range(EB)][-1],
                          reads=[f"w1sl{j2 % 2}", "xn_fm"], writes=[f"pb{bk}"], name="h_mm")
                        S("act", lambda a, hb=hb, bk=bk: a.activation(
                            h_fm[:, 512 * hb:512 * hb + 512], pb[bk][:], AF.Gelu,
                            bias=b1_col[:, hb:hb + 1]),
                          reads=[f"pb{bk}", "b1c"], writes=["h", f"pb{bk}"], name="hev")
                for hb in range(32 if l0_lvl >= 3 else 0):
                    buf = w2sl[hb % 2]
                    sp_dma([(buf[:], dcache["d_w2"].ap()[128 * hb:128 * hb + 128, :])],
                           reads=["d_w2", f"w2sl{hb % 2}"], writes=[f"w2sl{hb % 2}"],
                           name="w2sl")
                    S("pe", lambda t, buf=buf, hb=hb: [
                        t.matmul(bank_ap(2 * tb + fc),
                                 h_fm[:, 512 * hb + 128 * tb:512 * hb + 128 * tb + 128],
                                 buf[:, 512 * fc:512 * fc + 512],
                                 start=(hb == 0), stop=(hb == 31))
                        for tb in range(RB) for fc in range(2)][-1],
                      reads=[f"w2sl{hb % 2}", "h"],
                      writes=[bank_res(g) for g in range(8)], name="z_mm")
                for tb in range(RB if l0_lvl >= 3 else 0):
                    for fc in range(2):
                        g_ = 2 * tb + fc
                        S("dve", lambda d, tb=tb, fc=fc, g_=g_: d.tensor_add(
                            scr_f[:, 1024 * tb + 512 * fc:1024 * tb + 512 * fc + 512],
                            bank_ap(g_), b2_bc[:, 512 * fc:512 * fc + 512]),
                          reads=[bank_res(g_), "b2bc", "scr"], writes=["scr"])
                        S("act", lambda a, tb=tb, fc=fc: a.activation(
                            g_t[:, 1024 * tb + 512 * fc:1024 * tb + 512 * fc + 512],
                            scr_f[:, 1024 * tb + 512 * fc:1024 * tb + 512 * fc + 512],
                            AF.Silu),
                          reads=["scr"], writes=["g", bank_res(g_)])

                if t_idx == 0:
                    DBG("h", h_fm[:, 0:4096], "h")
                    DBG("g", g_t[:], "g")
                if t_idx == 0:
                    DBG("ygt", y_t[:], "y")
                transpose_tile(y_t, yg_fm, "y", "yg")

                for fc in range(2):
                    buf = stream_wt("d_wo", fc, tag="wo")
                    for tb in range(RB):
                        bk = tb % 2
                        S("pe", lambda t, buf=buf, tb=tb, bk=bk: [
                            t.matmul(pb[bk][:],
                                     yg_fm[:, 512 * eb + 128 * tb:512 * eb + 128 * tb + 128],
                                     buf[:, 512 * eb:512 * eb + 512],
                                     start=(eb == 0), stop=(eb == EB - 1))
                            for eb in range(EB)][-1],
                          reads=["yg", f"wbuf{fc % 2}"], writes=[f"pb{bk}"], name="wo_mm")
                        S("dve", lambda d, tb=tb, fc=fc, bk=bk: d.tensor_mul(
                            scr_f[:, 1024 * tb + 512 * fc:1024 * tb + 512 * fc + 512],
                            g_t[:, 1024 * tb + 512 * fc:1024 * tb + 512 * fc + 512],
                            pb[bk][:]),
                          reads=[f"pb{bk}", "g", "scr"], writes=["scr"])
                        S("dve", lambda d, tb=tb, fc=fc, bk=bk: d.tensor_add(
                            x1_t[:, 1024 * tb + 512 * fc:1024 * tb + 512 * fc + 512],
                            x_t[:, 1024 * tb + 512 * fc:1024 * tb + 512 * fc + 512],
                            scr_f[:, 1024 * tb + 512 * fc:1024 * tb + 512 * fc + 512]),
                          reads=["scr", "x"], writes=["x1", f"pb{bk}"])
            else:
                for rb in range(RB):
                    S("dve", lambda d, rb=rb: d.tensor_copy(
                        x1_t[:, 1024 * rb:1024 * rb + 1024],
                        x_t[:, 1024 * rb:1024 * rb + 1024]),
                      reads=["x"], writes=["x1"])

            if t_idx + 1 < n_tiles:
                nb, nch_ = divmod(t_idx + 1, NCH)
                nrow0 = nb * L + nch_ * 512
                sp_dma([(x_t[:].rearrange("p (a f) -> p a f", f=D),
                         x_in.ap()[nrow0:nrow0 + 512, :].rearrange("(a p) f -> p a f", p=128))],
                       reads=["x"], writes=["x"], name="xpref")

            if 1 not in layers and t_idx + 1 < n_tiles and 0 in layers and l0_lvl >= 1:
                l0_prolog(t_idx + 1)

            if 1 in layers:
                ln_stats(x1_t, "x1", 0, ST, STR, scr_f, "scr")
                for rb in range(RB):
                    S("act", lambda a, rb=rb, ST=ST: a.activation(
                        y_t[:, 1024 * rb:1024 * rb + 1024],
                        x1_t[:, 1024 * rb:1024 * rb + 1024],
                        AF.Identity, scale=ST[:, 4 + rb:5 + rb],
                        bias=ST[:, 8 + rb:9 + rb]),
                      reads=["x1", STR, "y"], writes=["y"], name="ln2ap")
                transpose_tile(y_t, xn1_fm, "y", "xn1_fm")

                for fc in range(4):
                    buf = stream_wt("d_wqkv", fc, tag="qk")
                    for fb in range(4):
                        fglob = 512 * fc + 128 * fb
                        bk = fb % 2
                        S("pe", lambda t, buf=buf, fb=fb, bk=bk: [
                            t.matmul(pb[bk][:],
                                     buf[:, 512 * eb + 128 * fb:512 * eb + 128 * fb + 128],
                                     xn1_fm[:, 512 * eb:512 * eb + 512],
                                     start=(eb == 0), stop=(eb == EB - 1))
                            for eb in range(EB)][-1],
                          reads=[f"wbuf{fc % 2}", "xn1_fm"], writes=[f"pb{bk}"],
                          name="qk_mm")
                        if fglob < 1024:
                            ebq = fglob // 128
                            S("act", lambda a, ebq=ebq, bk=bk: a.activation(
                                yg_fm[:, 512 * ebq:512 * ebq + 512], pb[bk][:],
                                AF.Identity, scale=0.125),
                              reads=[f"pb{bk}"], writes=["q1", f"pb{bk}"])
                        else:
                            ebk = (fglob - 1024) // 128
                            S("act", lambda a, ebk=ebk, bk=bk: a.activation(
                                v_t[:, 512 * ebk:512 * ebk + 512], pb[bk][:], AF.Copy),
                              reads=[f"pb{bk}"], writes=["k1", f"pb{bk}"])
                for fc in range(2):
                    buf = stream_wt("d_wqkv", 4 + fc, tag="v1")
                    for tb in range(RB):
                        bk = tb % 2
                        S("pe", lambda t, buf=buf, tb=tb, bk=bk: [
                            t.matmul(pb[bk][:],
                                     xn1_fm[:, 512 * eb + 128 * tb:512 * eb + 128 * tb + 128],
                                     buf[:, 512 * eb:512 * eb + 512],
                                     start=(eb == 0), stop=(eb == EB - 1))
                            for eb in range(EB)][-1],
                          reads=["xn1_fm", f"wbuf{fc % 2}"], writes=[f"pb{bk}"],
                          name="v1_mm")
                        S("act", lambda a, tb=tb, fc=fc, bk=bk: a.activation(
                            q_t[:, 1024 * tb + 512 * fc:1024 * tb + 512 * fc + 512],
                            pb[bk][:], AF.Copy),
                          reads=[f"pb{bk}"], writes=["v1", f"pb{bk}"])


                def _n(e, ins):
                    e.nop()
                    return ins
                S("dve", lambda d, ST=ST: _n(d, d.tensor_mul(ST[:, 48:52], ST[:, 20:24],
                                                             ST[:, 4:8])),
                  reads=[STR], writes=[STR], small=True)
                S("dve", lambda d, ST=ST: _n(d, d.tensor_mul(ST[:, 60:64], ST[:, 48:52],
                                                             ST[:, 4:8])),
                  reads=[STR], writes=[STR], small=True)
                S("act", lambda a, ST=ST: _n(a, a.activation(ST[:, 52:56], ST[:, 60:64],
                                                             AF.Sqrt, bias=EPS)),
                  reads=[STR], writes=[STR], small=True)
                S("dve", lambda d, ST=ST: _n(d, d.reciprocal(ST[:, 56:60], ST[:, 52:56])),
                  reads=[STR], writes=[STR], small=True)
                S("pe", lambda t, ST=ST: [
                    t.transpose(pb[1][0:1, 128 * rb:128 * rb + 128],
                                ST[:, 56 + rb:57 + rb], eye_f[:])
                    for rb in range(RB)][-1],
                  reads=[STR, "eye", "pb1"], writes=["pb1"], name="r3tr")
                S("act", lambda a: a.activation(rd_row[0][:], pb[1][0:1, 0:512], AF.Copy),
                  reads=["pb1"], writes=["rr0", "pb1"], name="r3ev", small=True)
                S("pe", lambda t: t.matmul(pb[1][:], ones_bf[:], rd_row[0][:],
                                           start=True, stop=True),
                  reads=["rr0", "pb1"], writes=["pb1"], name="r3bc")
                for eb in range(EB):
                    S("dve", lambda d, eb=eb: d.tensor_mul(
                        scr_f[:, 512 * eb:512 * eb + 512],
                        ln3_fm[:, 512 * eb:512 * eb + 512], pb[1][:]),
                      reads=["xn1_fm", "pb1", "scr"], writes=["scr"], name="l3mul")
                for eb in range(EB):
                    S("act", lambda a, eb=eb: a.activation(
                        ln3_fm[:, 512 * eb:512 * eb + 512],
                        scr_f[:, 512 * eb:512 * eb + 512],
                        AF.Identity, scale=g1_col[:, eb:eb + 1], bias=bt1_col[:, eb:eb + 1]),
                      reads=["scr", "g1c", "bt1c"], writes=["xn1_fm"], name="l3ap")

                for j2 in range(16):
                    buf = w1sl[j2 % 2]
                    sp_dma([(buf[:].rearrange("p (a f) -> p a f", f=256),
                             dcache["d_w1d"].ap()[:, 256 * j2:256 * j2 + 256]
                             .rearrange("(a p) f -> p a f", p=128))],
                           reads=["d_w1d", f"w1sl{j2 % 2}"], writes=[f"w1sl{j2 % 2}"],
                           name="w1dsl")
                    for k in range(2):
                        hb = 2 * j2 + k
                        bk = hb % 2
                        S("pe", lambda t, buf=buf, k=k, bk=bk: [
                            t.matmul(pb[bk][:],
                                     buf[:, 256 * eb + 128 * k:256 * eb + 128 * k + 128],
                                     ln3_fm[:, 512 * eb:512 * eb + 512],
                                     start=(eb == 0), stop=(eb == EB - 1))
                            for eb in range(EB)][-1],
                          reads=[f"w1sl{j2 % 2}", "xn1_fm"], writes=[f"pb{bk}"], name="u_mm")
                        S("act", lambda a, hb=hb, bk=bk: a.activation(
                            h_fm[:, 512 * hb:512 * hb + 512], pb[bk][:], AF.Silu,
                            bias=fc1b_col[:, hb:hb + 1]),
                          reads=[f"pb{bk}", "fc1c"], writes=["h", f"pb{bk}"], name="uev")
                for hb in range(32):
                    buf = w2sl[hb % 2]
                    sp_dma([(buf[:], dcache["d_w2d"].ap()[128 * hb:128 * hb + 128, :])],
                           reads=["d_w2d", f"w2sl{hb % 2}"], writes=[f"w2sl{hb % 2}"],
                           name="w2dsl")
                    S("pe", lambda t, buf=buf, hb=hb: [
                        t.matmul(bank_ap(2 * tb + fc),
                                 h_fm[:, 512 * hb + 128 * tb:512 * hb + 128 * tb + 128],
                                 buf[:, 512 * fc:512 * fc + 512],
                                 start=(hb == 0), stop=(hb == 31))
                        for tb in range(RB) for fc in range(2)][-1],
                      reads=[f"w2sl{hb % 2}", "h"],
                      writes=[bank_res(g) for g in range(8)], name="z2_mm")
                for tb in range(RB):
                    for fc in range(2):
                        g_ = 2 * tb + fc
                        S("dve", lambda d, tb=tb, fc=fc, g_=g_: d.scalar_tensor_tensor(
                            scr_f[:, 1024 * tb + 512 * fc:1024 * tb + 512 * fc + 512],
                            bank_ap(g_), 0.5, hfc2_bc[:, 512 * fc:512 * fc + 512],
                            ALU.mult, ALU.add),
                          reads=[bank_res(g_), "hfc2", "scr"], writes=["scr"])
                        S("dve", lambda d, tb=tb, fc=fc: d.tensor_add(
                            scr_f[:, 1024 * tb + 512 * fc:1024 * tb + 512 * fc + 512],
                            scr_f[:, 1024 * tb + 512 * fc:1024 * tb + 512 * fc + 512],
                            y_t[:, 1024 * tb + 512 * fc:1024 * tb + 512 * fc + 512]),
                          reads=["scr", "y"], writes=["scr"])
                        S("act", lambda a, tb=tb, fc=fc, g_=g_: a.activation(
                            g_t[:, 1024 * tb + 512 * fc:1024 * tb + 512 * fc + 512],
                            scr_f[:, 1024 * tb + 512 * fc:1024 * tb + 512 * fc + 512],
                            AF.Silu),
                          reads=["scr"], writes=["g", bank_res(g_)])

                if t_idx + 1 < n_tiles and 0 in layers and l0_lvl >= 1:
                    l0_prolog(t_idx + 1)

                def att_st(h):
                    ebh = h // 2
                    base = 64 * (h % 2)
                    par = h % 4
                    for kb in range(RB):
                        bk = (2 * (h % 2)) + (kb % 2)
                        S("pe", lambda t, h=h, kb=kb, bk=bk, ebh=ebh, base=base: t.matmul(
                            pb[bk][:],
                            v_t[base:base + 64,
                                512 * ebh + 128 * kb:512 * ebh + 128 * kb + 128],
                            yg_fm[base:base + 64, 512 * ebh:512 * ebh + 512],
                            start=True, stop=True),
                          reads=["k1", "q1", f"pb{bk}"], writes=[f"pb{bk}"], name="st_mm")
                        lam_col = 16 * (4 * ch + kb) + h
                        S("act", lambda a, kb=kb, par=par, bk=bk, lam_col=lam_col:
                          a.activation(
                              h_fm[:, 2048 * par + 512 * kb:2048 * par + 512 * kb + 512],
                              pb[bk][:], AF.Exp, scale=lamT[:, lam_col:lam_col + 1]),
                          reads=[f"pb{bk}", "lamT", f"at{par}"],
                          writes=[f"at{par}", f"pb{bk}"], name="expev")

                def att_dv(h):
                    par = h % 4
                    base = 64 * (h % 2)
                    S("pe", lambda t, h=h, par=par: [
                        t.matmul(pb[4][0:16, :], e16_sb[:, 16 * h:16 * h + 16],
                                 h_fm[:, 2048 * par + 512 * kb:2048 * par + 512 * kb + 512],
                                 start=(h == 0 and kb == 0),
                                 stop=(h == NH - 1 and kb == RB - 1))
                        for kb in range(RB)][-1],
                      reads=[f"at{par}", "e16", "pb4"], writes=["pb4"], name="dn_mm")
                    bk5 = 5 + (h % 4) // 2
                    S("pe", lambda t, h=h, par=par, base=base, bk5=bk5: [
                        t.matmul(pb[bk5][base:base + 64, :],
                                 q_t[:, 1024 * kb + 64 * h:1024 * kb + 64 * h + 64],
                                 h_fm[:, 2048 * par + 512 * kb:2048 * par + 512 * kb + 512],
                                 start=(kb == 0), stop=(kb == RB - 1),
                                 tile_position=(0, base) if base else None)
                        for kb in range(RB)][-1],
                      reads=[f"at{par}", "v1", f"pb{bk5}"], writes=[f"pb{bk5}"],
                      name="av_mm")
                    if h % 4 == 3:
                        ebp = h // 4
                        S("act", lambda a, ebp=ebp: [
                            a.activation(
                                xn_fm[:, 512 * (2 * ebp + i):512 * (2 * ebp + i) + 512],
                                pb[5 + i][:], AF.Copy) for i in range(2)][-1],
                          reads=["pb5", "pb6"], writes=["att", "pb5", "pb6"],
                          name="attev")

                att_st(0)
                att_st(1)
                for h in range(NH):
                    if h + 2 < NH:
                        att_st(h + 2)
                    att_dv(h)
                S("dve", lambda d: d.reciprocal(scr_f[0:16, 0:512], pb[4][0:16, :]),
                  reads=["pb4", "scr"], writes=["scr", "pb4"])
                S("act", lambda a: a.activation(dn_sb[:], scr_f[0:16, 0:512], AF.Copy),
                  reads=["scr"], writes=["dn"], small=True)
                pl_dma([(dcache["d_dn"].ap(), dn_sb[:])], reads=["dn"], writes=["d_dn"])
                for h in range(NH):
                    rr = rd_row[h % 2]
                    pl_dma([(rr[:], dcache["d_dn"].ap()[h:h + 1, :])],
                           reads=["d_dn", f"rr{h % 2}"], writes=[f"rr{h % 2}"])
                    base = 64 * (h % 2)
                    S("pe", lambda t, rr=rr, base=base: t.matmul(
                        pb[4][base:base + 64, :], ones_bf[0:1, 0:64], rr[:],
                        start=True, stop=True,
                        tile_position=(0, base) if base else None),
                      reads=[f"rr{h % 2}", "pb4"], writes=["pb4"], name="rd_mm")
                    if h % 2 == 1:
                        ebh2 = h // 2
                        S("dve", lambda d, ebh2=ebh2: d.tensor_mul(
                            xn_fm[:, 512 * ebh2:512 * ebh2 + 512],
                            xn_fm[:, 512 * ebh2:512 * ebh2 + 512], pb[4][:]),
                          reads=["att", "pb4"], writes=["att", "pb4"])
                for fc in range(2):
                    buf = stream_wt("d_wout", fc, tag="wop")
                    for tb in range(RB):
                        bk = tb % 2
                        S("pe", lambda t, buf=buf, tb=tb, bk=bk: [
                            t.matmul(pb[bk][:],
                                     xn_fm[:, 512 * eb + 128 * tb:512 * eb + 128 * tb + 128],
                                     buf[:, 512 * eb:512 * eb + 512],
                                     start=(eb == 0), stop=(eb == EB - 1))
                            for eb in range(EB)][-1],
                          reads=["att", f"wbuf{fc % 2}"], writes=[f"pb{bk}"],
                          name="wout_mm")
                        S("dve", lambda d, tb=tb, fc=fc, bk=bk: d.tensor_add(
                            scr_f[:, 1024 * tb + 512 * fc:1024 * tb + 512 * fc + 512],
                            pb[bk][:], bout_bc[:, 512 * fc:512 * fc + 512]),
                          reads=[f"pb{bk}", "boutb", "scr"], writes=["scr"])
                        S("dve", lambda d, tb=tb, fc=fc: d.tensor_mul(
                            scr_f[:, 1024 * tb + 512 * fc:1024 * tb + 512 * fc + 512],
                            scr_f[:, 1024 * tb + 512 * fc:1024 * tb + 512 * fc + 512],
                            g_t[:, 1024 * tb + 512 * fc:1024 * tb + 512 * fc + 512]),
                          reads=["scr", "g"], writes=["scr"])
                        S("dve", lambda d, tb=tb, fc=fc, bk=bk: d.tensor_add(
                            x1_t[:, 1024 * tb + 512 * fc:1024 * tb + 512 * fc + 512],
                            x1_t[:, 1024 * tb + 512 * fc:1024 * tb + 512 * fc + 512],
                            scr_f[:, 1024 * tb + 512 * fc:1024 * tb + 512 * fc + 512]),
                          reads=["scr", "x1"], writes=["x1", f"pb{bk}"])

            pl_dma([(x_out.ap()[row0:row0 + 512, :].rearrange("(a p) f -> p a f", p=128),
                     x1_t[:].rearrange("p (a f) -> p a f", f=D))],
                   reads=["x1"], writes=["xoutd"], name="out")

    for q, host in (("sp", "sp"), ("pl", "pl"), ("aq", "act")):
        fins = P.queue_fin(q)

        def mkfin(fins):
            def f(obj):
                for sem, val in fins:
                    obj.wait_ge(sem, val)
            return f
        P.steps[host].append(("q", mkfin(fins), {}, "fin", None))

    with nc.Block() as blk:
        @blk.sync
        def _(sp):
            P.emit_engine("sp", sp)

        @blk.gpsimd
        def _(g):
            P.emit_engine("pl", g)

        @blk.tensor
        def _(t):
            P.emit_engine("pe", t)

        @blk.scalar
        def _(a):
            P.emit_engine("act", a)

        @blk.vector
        def _(d):
            P.emit_engine("dve", d)

    return nc


_NC_CACHE = {}


def kernel(**inputs):
    layers = (0, 1)
    n_batch = 2
    L = 4096
    key = (layers, n_batch, L)
    if key not in _NC_CACHE:
        _NC_CACHE[key] = build_program(layers, n_batch, L)
    nc = _NC_CACHE[key]

    x = np.ascontiguousarray(inputs["L0_attn_x"])
    in_maps = []
    for c in range(N_CORES):
        m = {"x_in": x[n_batch * c:n_batch * (c + 1)].reshape(n_batch * L, D)}
        for k, v in inputs.items():
            if k != "L0_attn_x":
                m[k] = np.asarray(v)
        in_maps.append(m)
    res = run_bass_kernel_spmd(nc, in_maps, core_ids=list(range(N_CORES)))
    out = np.concatenate([r["x_out"].reshape(n_batch, L, D) for r in res.results],
                         axis=0)
    return out.astype(np.float32, copy=False)


# version 07
# speedup vs baseline: 1.0156x; 1.0156x over previous
import numpy as np
import ml_dtypes

import concourse.bass as bass
import concourse.mybir as mybir
from concourse.bass_utils import run_bass_kernel_spmd

F32 = mybir.dt.float32
BF16 = mybir.dt.bfloat16
I32 = mybir.dt.int32
AF = mybir.ActivationFunctionType
ALU = mybir.AluOpType
AX = mybir.AxisListType

D = 1024
NH = 16
DH = 64
FF = 4096
EPS = 1e-5
N_CORES = 8
RB = 4
EB = 8

BF16_NP = ml_dtypes.bfloat16


class Prog:

    ENGS = ("sp", "pl", "aq", "pe", "act", "dve")
    QUEUES = ("sp", "pl", "aq")
    W = 8

    def __init__(self, nc):
        self.nc = nc
        self.sems = {e: nc.alloc_semaphore(f"s_{e}") for e in ("pe", "act", "dve")}
        self.qsems = {q: [nc.alloc_semaphore(f"s_{q}{i}") for i in range(self.W)]
                      for q in self.QUEUES}
        self.steps = {e: [] for e in self.ENGS}
        self.cum = {e: 0 for e in ("pe", "act", "dve")}
        self.qcum = {q: [0] * self.W for q in self.QUEUES}
        self.qidx = {q: 0 for q in self.QUEUES}
        self.res = {}
        self.small_sids = set()

    def step(self, eng, emit, reads=(), writes=(), name="", small=False):
        deps = {}

        def _add(sid):
            if sid is None:
                return
            if sid[0] == "c":
                if sid[1] != eng or sid in self.small_sids:
                    key = ("c", sid[1])
                    deps[key] = max(deps.get(key, 0), sid[2])
            else:
                if sid[1] != eng:
                    key = ("q", sid[1], sid[2])
                    deps[key] = max(deps.get(key, 0), sid[3])

        for r in reads:
            ent = self.res.get(r)
            if ent:
                _add(ent[0])
        for r in writes:
            ent = self.res.get(r)
            if ent:
                _add(ent[0])
                for rd in ent[1]:
                    _add(rd)
        self.cum[eng] += 1
        sid = ("c", eng, self.cum[eng])
        if small:
            self.small_sids.add(sid)
        self.steps[eng].append(("c", emit, deps, name, None))
        self._commit(sid, reads, writes)
        return sid

    def dma(self, q, outs_ins, reads=(), writes=(), name="", transpose=False):
        deps = {}

        def _add(sid):
            if sid is None:
                return
            if sid[0] == "c":
                key = ("c", sid[1])
                deps[key] = max(deps.get(key, 0), sid[2])
            else:
                key = ("q", sid[1], sid[2])
                deps[key] = max(deps.get(key, 0), sid[3])

        for r in reads:
            ent = self.res.get(r)
            if ent:
                _add(ent[0])
        for r in writes:
            ent = self.res.get(r)
            if ent:
                _add(ent[0])
                for rd in ent[1]:
                    _add(rd)
        slot = self.qidx[q] % self.W
        self.qidx[q] += 1
        before = self.qcum[q][slot]
        self.qcum[q][slot] += 16 * len(outs_ins)
        sid = ("q", q, slot, self.qcum[q][slot])
        sem = self.qsems[q][slot]

        def emit(obj, outs_ins=outs_ins, sem=sem, transpose=transpose):
            for o, i in outs_ins:
                obj.dma_start(o, i, transpose=transpose).then_inc(sem, 16)

        host = "act" if q == "aq" else q
        self.steps[host].append(("q", emit, deps, name, (q, slot, before)))
        self._commit(sid, reads, writes)
        return sid

    def _commit(self, sid, reads, writes):
        for r in writes:
            self.res[r] = [sid, []]
        for r in reads:
            ent = self.res.setdefault(r, [None, []])
            ent[1].append(sid)

    def emit_engine(self, eng, obj):
        waited = {}
        for kind, emit, deps, name, qinfo in self.steps[eng]:
            if qinfo is not None:
                qq, slot, before = qinfo
                if before > 0:
                    obj.wait_ge(self.qsems[qq][slot], before)
            for key, dv in sorted(deps.items(), key=str):
                if dv > waited.get(key, 0):
                    sem = (self.sems[key[1]] if key[0] == "c"
                           else self.qsems[key[1]][key[2]])
                    obj.wait_ge(sem, dv)
                    waited[key] = dv
            last = emit(obj)
            if eng in ("pe", "act", "dve") and last is not None:
                last.then_inc(self.sems[eng], 1)

    def queue_fin(self, q):
        return [(self.qsems[q][s], self.qcum[q][s]) for s in range(self.W)
                if self.qcum[q][s] > 0]


def _log_pos_const(L):
    lp = np.zeros((2, L), np.float32)
    lp[0] = np.log(np.arange(L, dtype=np.float64) + 1.0).astype(np.float32)
    lp[1] = 1.0
    return lp


def build_program(layers=(0, 1), n_batch=2, L=4096, detect_races=False, dbg=False, l0_lvl=99, ph0=True, setup=True):
    NCH = L // 512
    n_tiles = n_batch * NCH
    nc = bass.Bass("TRN2", target_bir_lowering=False,
                   detect_race_conditions=detect_races)

    x_in = nc.dram_tensor("x_in", [n_batch * L, D], F32, kind="ExternalInput")
    x_out = nc.dram_tensor("x_out", [n_batch * L, D], F32, kind="ExternalOutput")

    win = {}
    def WIN(name, shape, dtype=F32):
        win[name] = nc.dram_tensor(name, shape, dtype, kind="ExternalInput")

    if 0 in layers:
        WIN("L0_attn_Wq", [D, D]); WIN("L0_attn_Wv", [D, D])
        WIN("L0_attn_Wg", [D, D]); WIN("L0_attn_Wo", [D, D])
        WIN("L0_attn_beta", [NH]); WIN("L0_attn_ln_gamma", [D]); WIN("L0_attn_ln_beta", [D])
        WIN("L0_ffn_w1", [D, FF]); WIN("L0_ffn_b1", [FF])
        WIN("L0_ffn_w2", [FF, D]); WIN("L0_ffn_b2", [D])
    if 1 in layers:
        WIN("L1_attn_Wqkv", [3 * D, D]); WIN("L1_attn_Wout", [D, D]); WIN("L1_attn_bout", [D])
        WIN("L1_attn_Wl1", [64, 1]); WIN("L1_attn_bl1", [64])
        WIN("L1_attn_Wl2", [NH, 64]); WIN("L1_attn_bl2", [NH])
        WIN("L1_ffn_ln_gamma", [D]); WIN("L1_ffn_ln_beta", [D])
        WIN("L1_ffn_fc1_q", [D, FF], I32); WIN("L1_ffn_fc1_scales", [8, FF])
        WIN("L1_ffn_fc1_bias", [FF])
        WIN("L1_ffn_fc2_q", [FF, D], I32); WIN("L1_ffn_fc2_scales", [32, D])
        WIN("L1_ffn_fc2_bias", [D])

    dbg_out = {}
    def DBG(name, buf_ap, res, cols=4096, dt=BF16):
        if not dbg:
            return
        t = nc.dram_tensor(f"dbg_{name}", [128, cols], dt, kind="ExternalOutput")
        dbg_out[name] = t
        pl_dma_ref[0]([(t.ap(), buf_ap)], reads=[res], writes=[f"dbgd_{name}"], name=f"dbg{name}")

    pl_dma_ref = [None]
    dcache = {}
    def DC(name, shape, dtype=BF16):
        dcache[name] = nc.dram_tensor(name, shape, dtype, kind="Internal")

    if 0 in layers:
        DC("d_wq", [D, D]); DC("d_wv", [D, D]); DC("d_wg", [D, D]); DC("d_wo", [D, D])
        DC("d_wqT", [D, D]); DC("d_wvT", [D, D]); DC("d_wgT", [D, D]); DC("d_woT", [D, D])
        DC("d_w1", [D, FF]); DC("d_w2", [FF, D])
    if 1 in layers:
        DC("d_wqkv", [3 * D, D]); DC("d_wout", [D, D])
        DC("d_wqkvT", [D, 3 * D]); DC("d_woutT", [D, D])
        DC("d_w1d", [D, FF]); DC("d_w2d", [FF, D])
        DC("d_dn", [NH, 512], BF16)

    eye_bf_d = nc.inline_tensor(np.eye(128).astype(BF16_NP), "c_eye_bf")
    eye_f_d = nc.inline_tensor(np.eye(128, dtype=np.float32), "c_eye_f")
    if 0 in layers:
        jj, ii = np.mgrid[0:128, 0:128]
        delta_d = nc.inline_tensor((ii - jj).astype(np.float32), "c_delta")
        tri01_d = nc.inline_tensor((ii >= jj).astype(np.float32), "c_tri01")
        trineg_d = nc.inline_tensor(
            np.where(ii >= jj, 0.0, -30000.0).astype(np.float32), "c_trineg")
    if 1 in layers:
        lp_d = nc.inline_tensor(_log_pos_const(L), "c_logpos")
        e16 = np.zeros((NH, 128, NH), np.float32)
        for h in range(NH):
            e16[h, :, h] = 1.0
        e16_d = nc.inline_tensor(e16.reshape(NH * 128, NH).astype(BF16_NP), "c_e16")

    SB = nc.alloc_sbuf_tensor
    eye_bf = SB("eye_bf", [128, 128], BF16)
    eye_f = SB("eye_f", [128, 128], F32)
    ones_bf = SB("ones_bf", [1, 128], BF16)
    ones_f = SB("ones_f", [1, 128], F32)
    cst_eps = SB("cst_eps", [128, 1], F32)
    stats = SB("stats", [128, 64], F32)
    stats_b = SB("stats_b", [128, 64], F32)
    x_t = SB("x_t", [128, 4 * D], F32)
    x1_t = SB("x1_t", [128, 4 * D], F32)
    scr_f = SB("scr_f", [128, 4 * D], F32)
    xn_tmp = SB("xn_tmp", [128, 4 * D], BF16)
    xn_fm = SB("xn_fm", [128, 8 * 512], BF16)
    h_fm = SB("h_fm", [128, 32 * 512], BF16)
    g_t = SB("g_t", [128, 4 * D], BF16)
    q_t = SB("q_t", [128, 4 * D], BF16)
    v_t = SB("v_t", [128, 4 * D], BF16)
    y_t = SB("y_t", [128, 4 * D], BF16)
    g0_t = SB("g0_t", [128, 4 * D], BF16)
    yg_fm = SB("yg_fm", [128, 8 * 512], BF16)
    xn1_fm = SB("xn1_fm", [128, 8 * 512], BF16)
    ln3_fm = None
    ln3_fm = xn1_fm
    wbuf = [SB(f"wbuf{i}", [128, 8 * 512], BF16) for i in range(2)]
    w1sl = [SB(f"w1sl{i}", [128, 2048], BF16) for i in range(2)]
    w2sl = [SB(f"w2sl{i}", [128, 1024], BF16) for i in range(2)]
    b2_bc = SB("b2_bc", [128, D], BF16)
    bout_bc = SB("bout_bc", [128, D], BF16)
    g0g_bc = SB("g0g_bc", [128, D], BF16)
    b0b_bc = SB("b0b_bc", [128, D], BF16)
    hfc2_bc = SB("hfc2_bc", [128, D], BF16)
    b1_col = SB("b1_col", [128, 32], F32)
    fc1b_col = SB("fc1b_col", [128, 32], F32)
    g1_col = SB("g1_col", [128, 8], F32)
    bt1_col = SB("bt1_col", [128, 8], F32)
    lt_all = SB("lt_all", [128, NH * 128], BF16)
    lam_row = SB("lam_row", [1, NH], F32)
    lnlam_cols = SB("lnlam_cols", [128, NH], F32)
    carry_sb = SB("carry_sb", [1, D], BF16)
    lamT = SB("lamT", [128, (L // 128) * NH], F32)
    wl1b = SB("wl1b", [2, 64], F32)
    wl2t = SB("wl2t", [64, 16], F32)
    wl2_sb = SB("wl2_sb", [16, 64], F32)
    bl2c = SB("bl2c", [16, 1], F32)
    e16_sb = SB("e16_sb", [128, 16 * NH], BF16)
    dn_sb = SB("dn_sb", [16, 512], BF16)
    rd_row = [SB(f"rd_row{i}", [1, 512], BF16) for i in range(2)]
    scrow = [SB(f"scrow{i}", [1, 512], BF16) for i in range(2)]
    misc_r = SB("misc_r", [1, 64], F32)
    delta_sb = x1_t[:, 0:128]
    tri01_sb = x1_t[:, 128:256]
    trineg_sb = x1_t[:, 256:384]

    PS = nc.alloc_psum_tensor
    pb = [PS(f"pb{i}", [128, 512], F32) for i in range(7)]
    pbt = PS("pbt", [128, 1024], BF16)
    carry_ps = pbt[0:1, :].bitcast(F32)
    pbt_f32 = pbt[:, :].bitcast(F32)

    def bank_ap(g):
        return pb[g][:] if g < 7 else pbt_f32

    def bank_res(g):
        return f"pb{g}" if g < 7 else "pbt"

    with nc.Block() as blk0:
        @blk0.gpsimd
        def _(g):
            g.memset(cst_eps[:], EPS)
            g.memset(ones_bf[:], 1.0)
            g.memset(ones_f[:], 1.0)
            g.memset(carry_sb[:], 0.0)
    nc.const_aps.register(F32, EPS, cst_eps[:])

    P = Prog(nc)
    S = P.step

    def sp_dma(outs_ins, reads, writes, name=""):
        return P.dma("sp", outs_ins, reads, writes, name=name)

    XBAR_Q = "aq"

    def sp_dma_t(outs_ins, reads, writes, name=""):
        return P.dma(XBAR_Q, outs_ins, reads, writes, name=name, transpose=True)

    def pl_dma(outs_ins, reads, writes, name=""):
        return P.dma("pl", outs_ins, reads, writes, name=name)
    pl_dma_ref[0] = pl_dma

    def cast_weight(src, dst, R, C, tag):
        for si, r0 in enumerate(range(0, R, 128)):
            slot = si % 4
            buf = h_fm[:, slot * 4096:slot * 4096 + C]
            pl_dma([(buf, src.ap()[r0:r0 + 128, :])],
                   reads=[f"cstg{slot}"], writes=[f"cstg{slot}"], name=f"c_{tag}")
            sp_dma([(dst.ap()[r0:r0 + 128, :], buf)],
                   reads=[f"cstg{slot}"], writes=[f"d_{tag}"])

    def build_wt(bname, tname, FT):
        for fc in range(FT // 512):
            for eb in range(EB):
                stg = wbuf[0][:, 512 * (eb % 4):512 * (eb % 4) + 512]
                P.dma("aq", [(stg, dcache[bname].ap()[512 * fc:512 * fc + 512,
                                                      128 * eb:128 * eb + 128])],
                      reads=[f"d_{bname[2:]}", f"wt{eb % 4}"], writes=[f"wt{eb % 4}"],
                      name="wtx", transpose=True)
                sp_dma([(dcache[tname].ap()[128 * eb:128 * eb + 128,
                                            512 * fc:512 * fc + 512], stg)],
                       reads=[f"wt{eb % 4}"], writes=[f"d_{tname[2:]}"], name="wts")

    if 0 in layers and ph0:
        cast_weight(win["L0_ffn_w1"], dcache["d_w1"], D, FF, "w1")
        cast_weight(win["L0_ffn_w2"], dcache["d_w2"], FF, D, "w2")
        cast_weight(win["L0_attn_Wq"], dcache["d_wq"], D, D, "wq")
        cast_weight(win["L0_attn_Wv"], dcache["d_wv"], D, D, "wv")
        cast_weight(win["L0_attn_Wg"], dcache["d_wg"], D, D, "wg")
        build_wt("d_wq", "d_wqT", D)
        build_wt("d_wv", "d_wvT", D)
        build_wt("d_wg", "d_wgT", D)
        cast_weight(win["L0_attn_Wo"], dcache["d_wo"], D, D, "wo")
        build_wt("d_wo", "d_woT", D)
    if 1 in layers and ph0:
        cast_weight(win["L1_attn_Wqkv"], dcache["d_wqkv"], 3 * D, D, "wqkv")
        build_wt("d_wqkv", "d_wqkvT", 3 * D)
        cast_weight(win["L1_attn_Wout"], dcache["d_wout"], D, D, "wout")
        build_wt("d_wout", "d_woutT", D)

        def dequant(qsrc, ssrc, dst, R, C, tag):
            for si, r0 in enumerate(range(0, R, 128)):
                qb = x_t[:].bitcast(I32)[:, 0:C]
                pl_dma([(qb, qsrc.ap()[r0:r0 + 128, :])],
                       reads=["x"], writes=["x"], name=f"dq_{tag}")
                S("dve", lambda d, qb=qb, C=C: d.tensor_copy(scr_f[:, 0:C], qb),
                  reads=["x"], writes=["scr"])
                for ci, c0 in enumerate(range(0, C, 512)):
                    sr = scrow[ci % 2]
                    pl_dma([(sr[:], ssrc.ap()[si:si + 1, c0:c0 + 512])],
                           reads=[f"sr{ci % 2}"], writes=[f"sr{ci % 2}"])
                    bk = ci % 2
                    S("pe", lambda t, sr=sr, bk=bk: t.matmul(
                        pb[bk][:], ones_bf[:], sr[:], start=True, stop=True),
                      reads=[f"sr{ci % 2}"], writes=[f"pb{bk}"])
                    S("dve", lambda d, c0=c0, bk=bk: d.scalar_tensor_tensor(
                        g0_t[:, c0:c0 + 512], scr_f[:, c0:c0 + 512], -128.0,
                        pb[bk][:], ALU.add, ALU.mult),
                      reads=["scr", f"pb{bk}"], writes=["g0", f"pb{bk}"])
                sp_dma([(dst.ap()[r0:r0 + 128, :], g0_t[:, 0:C])],
                       reads=["g0"], writes=[f"d_{tag}"])

        dequant(win["L1_ffn_fc1_q"], win["L1_ffn_fc1_scales"], dcache["d_w1d"], D, FF, "w1d")
        dequant(win["L1_ffn_fc2_q"], win["L1_ffn_fc2_scales"], dcache["d_w2d"], FF, D, "w2d")

    S("dve", lambda d: d.nop(),
      reads=["wt0", "wt1", "wt2", "wt3"], writes=["wbuf0", "wbuf1"], name="wtfence")

    sp_dma([(eye_bf[:], eye_bf_d.ap()), (eye_f[:], eye_f_d.ap())],
           reads=[], writes=["eye"])

    def bc_row(src_ap_row, dst, dst_res, scale=1.0):
        for c0 in range(0, D, 512):
            row = scrow[0]
            pl_dma([(row[:], src_ap_row[0:1, c0:c0 + 512])],
                   reads=["sr0"], writes=["sr0"], name="bcr")
            S("pe", lambda t, row=row: t.matmul(pb[2][:], ones_bf[:], row[:],
                                                start=True, stop=True),
              reads=["sr0"], writes=["pb2"])
            S("act", lambda a, c0=c0, dst=dst, scale=scale: a.activation(
                dst[:, c0:c0 + 512], pb[2][:], AF.Copy, scale=scale),
              reads=["pb2"], writes=[dst_res, "pb2"])

    def col_build(src_ap, dst, n, dst_res):
        sp_dma([(scr_f[0:n, 0:128], src_ap.rearrange("(a p) -> a p", p=128))],
               reads=["scr"], writes=["scr"], name="colb")
        S("pe", lambda t, n=n: t.transpose(pb[3][:, 0:n], scr_f[0:n, 0:128],
                                           eye_f[0:n, 0:n]),
          reads=["scr", "eye"], writes=["pb3"])
        S("act", lambda a, n=n, dst=dst: a.activation(dst[:, 0:n], pb[3][:, 0:n], AF.Copy),
          reads=["pb3"], writes=[dst_res, "pb3"])

    def row1(ap):
        return ap.rearrange("(a b) -> a b", a=1)

    if 0 in layers and setup:
        sp_dma([(delta_sb, delta_d.ap()), (tri01_sb, tri01_d.ap()),
                (trineg_sb, trineg_d.ap()),
                (misc_r[0:1, 0:16], row1(win["L0_attn_beta"].ap()))],
               reads=["x1"], writes=["scanc", "x1"])
        S("act", lambda a: (a.activation(lam_row[:], misc_r[0:1, 0:16], AF.Sigmoid),
                            a.nop())[0],
          reads=["scanc"], writes=["lamr"], small=True)
        S("act", lambda a: (a.activation(misc_r[0:1, 16:32], lam_row[:], AF.Ln),
                            a.nop())[0],
          reads=["lamr", "scanc"], writes=["scanc"], small=True)
        for h in range(NH):
            S("pe", lambda t, h=h: t.matmul(pb[0][:, h:h + 1], ones_f[:],
                                            misc_r[0:1, 16 + h:17 + h],
                                            start=True, stop=True),
              reads=["scanc"], writes=["pb0"], small=True)
        S("act", lambda a: a.activation(lnlam_cols[:], pb[0][:, 0:NH], AF.Copy),
          reads=["pb0"], writes=["lnlamc", "pb0"], small=True)
        for h in range(NH):
            S("act", lambda a, h=h: (a.activation(scr_f[:, 0:128], delta_sb,
                                                  AF.Identity,
                                                  scale=lnlam_cols[:, h:h + 1]),
                                     a.nop())[0],
              reads=["lnlamc", "scanc", "scr", "x1"], writes=["scr"], small=True)
            S("dve", lambda d: (d.tensor_mul(scr_f[:, 128:256], scr_f[:, 0:128],
                                             tri01_sb), d.nop())[0],
              reads=["scr", "scanc", "x1"], writes=["scr"], small=True)
            S("dve", lambda d: (d.tensor_add(scr_f[:, 128:256], scr_f[:, 128:256],
                                             trineg_sb), d.nop())[0],
              reads=["scr", "scanc", "x1"], writes=["scr"], small=True)
            S("act", lambda a, h=h: a.activation(lt_all[:, 128 * h:128 * h + 128],
                                                 scr_f[:, 128:256], AF.Exp),
              reads=["scr"], writes=["lt"], small=True)
        bc_row(row1(win["L0_attn_ln_gamma"].ap()), g0g_bc, "g0g")
        bc_row(row1(win["L0_attn_ln_beta"].ap()), b0b_bc, "b0b")
        bc_row(row1(win["L0_ffn_b2"].ap()), b2_bc, "b2bc")
        col_build(win["L0_ffn_b1"].ap(), b1_col, 32, "b1c")

    if 1 in layers and setup:
        bc_row(row1(win["L1_attn_bout"].ap()), bout_bc, "boutb")
        bc_row(row1(win["L1_ffn_fc2_bias"].ap()), hfc2_bc, "hfc2", scale=0.5)
        col_build(win["L1_ffn_fc1_bias"].ap(), fc1b_col, 32, "fc1c")
        col_build(win["L1_ffn_ln_gamma"].ap(), g1_col, 8, "g1c")
        col_build(win["L1_ffn_ln_beta"].ap(), bt1_col, 8, "bt1c")
        sp_dma([(e16_sb[:].rearrange("p (a f) -> p a f", f=NH),
                 e16_d.ap().rearrange("(a p) f -> p a f", p=128))],
               reads=[], writes=["e16"])
        lp_sb = x_t
        sp_dma([(wl1b[0:1, :], win["L1_attn_Wl1"].ap().rearrange("a b -> b a")),
                (wl1b[1:2, :], row1(win["L1_attn_bl1"].ap())),
                (bl2c[:], win["L1_attn_Wl2"].ap()[0:0 + NH, 0:1] if False else
                 win["L1_attn_bl2"].ap().rearrange("(p a) -> p a", a=1)),
                (wl2_sb[:], win["L1_attn_Wl2"].ap())],
               reads=[], writes=["lamnc"])
        S("pe", lambda t: t.transpose(pb[4][0:64, 0:16], wl2_sb[:], eye_f[0:16, 0:16]),
          reads=["lamnc", "eye"], writes=["pb4"])
        S("act", lambda a: a.activation(wl2t[:, 0:16], pb[4][0:64, 0:16], AF.Copy),
          reads=["pb4"], writes=["wl2t", "pb4"], small=True)
        for half in range((L + 2047) // 2048):
            c0 = half * 2048
            cw = min(2048, L - c0)
            sp_dma([(lp_sb[0:2, 0:cw], lp_d.ap()[0:2, c0:c0 + cw])],
                   reads=["x"], writes=["x"])
            for j in range(cw // 128):
                bk = 4 + (j % 2)
                S("pe", lambda t, j=j, bk=bk: t.matmul(
                    pb[bk][0:64, 0:128], wl1b[:], lp_sb[0:2, 128 * j:128 * j + 128],
                    start=True, stop=True),
                  reads=["lamnc", "x"], writes=[f"pb{bk}"])
                S("act", lambda a, j=j, c0=c0, bk=bk: a.activation(
                    scr_f[0:64, c0 + 128 * j:c0 + 128 * j + 128],
                    pb[bk][0:64, 0:128], AF.Relu),
                  reads=[f"pb{bk}"], writes=["scr", f"pb{bk}"])
        for j in range(L // 512):
            S("pe", lambda t, j=j: t.matmul(pb[5][0:16, :], wl2t[:, 0:16],
                                            scr_f[0:64, 512 * j:512 * j + 512],
                                            start=True, stop=True),
              reads=["wl2t", "scr"], writes=["pb5"])
            S("act", lambda a, j=j: a.activation(x1_t[0:16, 512 * j:512 * j + 512],
                                                 pb[5][0:16, :], AF.Sigmoid,
                                                 bias=bl2c[0:16, 0:1]),
              reads=["pb5", "lamnc"], writes=["x1", "pb5"])
        for kb in range(L // 128):
            S("pe", lambda t, kb=kb: t.transpose(pb[6][:, 0:16],
                                                 x1_t[0:16, 128 * kb:128 * kb + 128],
                                                 eye_f[0:16, 0:16]),
              reads=["x1", "eye"], writes=["pb6"])
            S("act", lambda a, kb=kb: a.activation(lamT[:, 16 * kb:16 * kb + 16],
                                                   pb[6][:, 0:16], AF.Copy),
              reads=["pb6"], writes=["lamT", "pb6"])

    def ln_stats(src_tile, src_res, c, ST, STR, sq, sq_res):
        def _n(d, ins):
            d.nop()
            return ins
        for rb in range(RB):
            S("dve", lambda d, rb=rb: d.tensor_mul(
                sq[:, 1024 * rb:1024 * rb + 1024],
                src_tile[:, 1024 * rb:1024 * rb + 1024],
                src_tile[:, 1024 * rb:1024 * rb + 1024]),
              reads=[src_res, sq_res], writes=[sq_res])
        for rb in range(RB):
            S("dve", lambda d, rb=rb: _n(d, d.reduce_sum(
                ST[:, c + 16 + rb:c + 17 + rb],
                src_tile[:, 1024 * rb:1024 * rb + 1024], axis=AX.X)),
              reads=[src_res], writes=[STR], small=True)
            S("dve", lambda d, rb=rb: _n(d, d.reduce_sum(
                ST[:, c + 20 + rb:c + 21 + rb],
                sq[:, 1024 * rb:1024 * rb + 1024], axis=AX.X)),
              reads=[sq_res], writes=[STR], small=True)
        S("dve", lambda d: _n(d, d.tensor_scalar_mul(ST[:, c:c + 4],
                                                     ST[:, c + 16:c + 20], 1.0 / D)),
          reads=[STR], writes=[STR], small=True)
        S("dve", lambda d: _n(d, d.tensor_scalar_mul(ST[:, c + 12:c + 16],
                                                     ST[:, c + 20:c + 24], 1.0 / D)),
          reads=[STR], writes=[STR], small=True)
        S("dve", lambda d: _n(d, d.tensor_mul(ST[:, c + 24:c + 28], ST[:, c:c + 4],
                                              ST[:, c:c + 4])),
          reads=[STR], writes=[STR], small=True)
        S("dve", lambda d: _n(d, d.tensor_sub(ST[:, c + 20:c + 24],
                                              ST[:, c + 12:c + 16],
                                              ST[:, c + 24:c + 28])),
          reads=[STR], writes=[STR], small=True)
        S("act", lambda a: _n(a, a.activation(ST[:, c + 24:c + 28],
                                              ST[:, c + 20:c + 24],
                                              AF.Sqrt, bias=EPS)),
          reads=[STR], writes=[STR], small=True)
        S("dve", lambda d: _n(d, d.reciprocal(ST[:, c + 4:c + 8],
                                              ST[:, c + 24:c + 28])),
          reads=[STR], writes=[STR], small=True)
        S("dve", lambda d: _n(d, d.tensor_mul(ST[:, c + 8:c + 12], ST[:, c:c + 4],
                                              ST[:, c + 4:c + 8])),
          reads=[STR], writes=[STR], small=True)
        S("dve", lambda d: _n(d, d.tensor_scalar_mul(ST[:, c + 8:c + 12],
                                                     ST[:, c + 8:c + 12], -1.0)),
          reads=[STR], writes=[STR], small=True)

    h_f32 = h_fm[:].bitcast(F32)

    def stats_of(t):
        return (stats if t % 2 == 0 else stats_b), f"st{t % 2}"

    def l0_prolog(t):
        ST, STR = stats_of(t)
        ln_stats(x_t, "x", 0, ST, STR, h_f32[:, 0:4096], "h")
        for rb in range(RB):
            S("act", lambda a, rb=rb, ST=ST: a.activation(
                xn_tmp[:, 1024 * rb:1024 * rb + 1024],
                x_t[:, 1024 * rb:1024 * rb + 1024],
                AF.Identity, scale=ST[:, 4 + rb:5 + rb],
                bias=ST[:, 8 + rb:9 + rb]),
              reads=["x", STR, "xn_tmp"], writes=["xn_tmp"], name="ln0ap")

    def transpose_tile(src_bf, dst_fm, src_res, dst_res):
        for rb in range(RB):
            S("pe", lambda t, rb=rb: [
                t.transpose(pbt[:, 128 * eb:128 * eb + 128],
                            src_bf[:, 1024 * rb + 128 * eb:1024 * rb + 128 * eb + 128],
                            eye_bf[:]) for eb in range(EB)][-1],
              reads=[src_res, "eye", "pbt"], writes=["pbt"])
            S("dve", lambda d, rb=rb: [
                d.tensor_copy(dst_fm[:, 512 * eb + 128 * rb:512 * eb + 128 * rb + 128],
                              pbt[:, 128 * eb:128 * eb + 128]) for eb in range(EB)][-1],
              reads=["pbt"], writes=[dst_res])

    def stream_wt(dname, fc, tag=""):
        buf = wbuf[fc % 2]
        tsrc = dcache[dname + "T"]
        sp_dma([(buf[:].rearrange("p (a f) -> p a f", f=512),
                 tsrc.ap()[:, 512 * fc:512 * fc + 512]
                 .rearrange("(a p) f -> p a f", p=128))],
               reads=[f"d_{dname[2:]}T", f"wbuf{fc % 2}"],
               writes=[f"wbuf{fc % 2}"], name=tag)
        return buf

    def proj_tm(dname, src_fm, src_res, dst, dst_res, act_fn, act_kw=None, tag=""):
        for fc in range(2):
            buf = stream_wt(dname, fc, tag=tag)
            for tb in range(RB):
                bk = tb % 2
                S("pe", lambda t, buf=buf, tb=tb, bk=bk: [
                    t.matmul(pb[bk][:],
                             src_fm[:, 512 * eb + 128 * tb:512 * eb + 128 * tb + 128],
                             buf[:, 512 * eb:512 * eb + 512],
                             start=(eb == 0), stop=(eb == EB - 1)) for eb in range(EB)][-1],
                  reads=[src_res, f"wbuf{fc % 2}"], writes=[f"pb{bk}"], name=f"{tag}mm")
                kw = dict(act_kw or {})
                S("act", lambda a, dst=dst, tb=tb, fc=fc, bk=bk, kw=kw: a.activation(
                    dst[:, 1024 * tb + 512 * fc:1024 * tb + 512 * fc + 512],
                    pb[bk][:], act_fn, **kw),
                  reads=[f"pb{bk}"] + list((act_kw or {}).pop("_res", [])),
                  writes=[dst_res, f"pb{bk}"], name=f"{tag}ev")

    for b in range(n_batch):
        for ch in range(NCH):
            t_idx = b * NCH + ch
            row0 = b * L + ch * 512
            ST, STR = stats_of(t_idx)
            if t_idx == 0:
                sp_dma([(x_t[:].rearrange("p (a f) -> p a f", f=D),
                         x_in.ap()[row0:row0 + 512, :].rearrange("(a p) f -> p a f", p=128))],
                       reads=["x"], writes=["x"], name="xload")
                if 0 in layers and l0_lvl >= 1:
                    l0_prolog(0)

            if 0 in layers and l0_lvl >= 1:
                transpose_tile(xn_tmp, xn_fm, "xn_tmp", "xn_fm")

                for j2 in range(16 if l0_lvl >= 2 else 0):
                    buf = w1sl[j2 % 2]
                    sp_dma([(buf[:].rearrange("p (a f) -> p a f", f=256),
                             dcache["d_w1"].ap()[:, 256 * j2:256 * j2 + 256]
                             .rearrange("(a p) f -> p a f", p=128))],
                           reads=["d_w1", f"w1sl{j2 % 2}"], writes=[f"w1sl{j2 % 2}"],
                           name="w1sl")
                    for k in range(2):
                        hb = 2 * j2 + k
                        bk = hb % 2
                        S("pe", lambda t, buf=buf, k=k, bk=bk: [
                            t.matmul(pb[bk][:],
                                     buf[:, 256 * eb + 128 * k:256 * eb + 128 * k + 128],
                                     xn_fm[:, 512 * eb:512 * eb + 512],
                                     start=(eb == 0), stop=(eb == EB - 1))
                            for eb in range(EB)][-1],
                          reads=[f"w1sl{j2 % 2}", "xn_fm"], writes=[f"pb{bk}"], name="h_mm")
                        S("act", lambda a, hb=hb, bk=bk: a.activation(
                            h_fm[:, 512 * hb:512 * hb + 512], pb[bk][:], AF.Gelu,
                            bias=b1_col[:, hb:hb + 1]),
                          reads=[f"pb{bk}", "b1c"], writes=["h", f"pb{bk}"], name="hev")
                if l0_lvl >= 4:
                    proj_tm("d_wq", xn_fm, "xn_fm", q_t, "q", AF.Copy, tag="q")
                    proj_tm("d_wv", xn_fm, "xn_fm", v_t, "v", AF.Copy, tag="v")
                    proj_tm("d_wg", xn_fm, "xn_fm", g0_t, "g0", AF.Silu, tag="gp")

                if t_idx == 0:
                    DBG("q", q_t[:], "q")
                    DBG("v", v_t[:], "v")
                    DBG("g0", g0_t[:], "g0")
                if ch == 0 and l0_lvl >= 5:
                    S("dve", lambda d: d.memset(carry_sb[:], 0.0),
                      reads=["carrysb"], writes=["carrysb"])
                for half in range(2 if l0_lvl >= 5 else 0):
                    hs = range(8 * half, 8 * half + 8)
                    for rc in range(RB):
                        if rc == 0:
                            S("dve", lambda d, half=half: [
                                d.scalar_tensor_tensor(
                                    v_t[0:1, 64 * h:64 * h + 64],
                                    carry_sb[0:1, 64 * h:64 * h + 64],
                                    lam_row[0:1, h:h + 1],
                                    v_t[0:1, 64 * h:64 * h + 64],
                                    ALU.mult, ALU.add)
                                for h in range(8 * half, 8 * half + 8)][-1],
                              reads=["carrysb", "lamr", "v"], writes=["v"], small=True)
                        else:
                            S("dve", lambda d, half=half, rc=rc: [
                                d.scalar_tensor_tensor(
                                    v_t[0:1, 1024 * rc + 64 * h:1024 * rc + 64 * h + 64],
                                    carry_ps[0:1, 64 * (h % 8):64 * (h % 8) + 64],
                                    lam_row[0:1, h:h + 1],
                                    v_t[0:1, 1024 * rc + 64 * h:1024 * rc + 64 * h + 64],
                                    ALU.mult, ALU.add)
                                for h in range(8 * half, 8 * half + 8)][-1],
                              reads=["pbt", "lamr", "v"], writes=["v"], small=True)
                        sbk = 4 if rc % 2 == 0 else 0
                        S("pe", lambda t, half=half, rc=rc, sbk=sbk: [
                            ins for h in range(8 * half, 8 * half + 8)
                            for ins in (
                                t.matmul(pb[sbk][:, 64 * (h % 8):64 * (h % 8) + 64],
                                         lt_all[:, 128 * h:128 * h + 128],
                                         v_t[:, 1024 * rc + 64 * h:1024 * rc + 64 * h + 64],
                                         start=True, stop=True),
                                t.matmul(carry_ps[0:1, 64 * (h % 8):64 * (h % 8) + 64],
                                         lt_all[:, 128 * h + 127:128 * h + 128],
                                         v_t[:, 1024 * rc + 64 * h:1024 * rc + 64 * h + 64],
                                         start=True, stop=True))][-1],
                          reads=["v", "lt", f"pb{sbk}", "pbt"], writes=[f"pb{sbk}", "pbt"],
                          name="scan")
                        S("dve", lambda d, half=half, rc=rc, sbk=sbk: d.tensor_mul(
                            y_t[:, 1024 * rc + 512 * half:1024 * rc + 512 * half + 512],
                            q_t[:, 1024 * rc + 512 * half:1024 * rc + 512 * half + 512],
                            pb[sbk][:]),
                          reads=[f"pb{sbk}", "q"], writes=["y", f"pb{sbk}"])
                        if rc == RB - 1:
                            S("act", lambda a, half=half: a.activation(
                                carry_sb[0:1, 512 * half:512 * half + 512],
                                carry_ps[0:1, 0:512], AF.Copy),
                              reads=["pbt"], writes=["carrysb", "pbt"], small=True)

                if t_idx == 0:
                    DBG("y", y_t[:], "y")
                if l0_lvl < 7:
                    for rb in range(RB):
                        S("dve", lambda d, rb=rb: d.tensor_copy(
                            x1_t[:, 1024 * rb:1024 * rb + 1024],
                            x_t[:, 1024 * rb:1024 * rb + 1024]),
                          reads=["x"], writes=["x1"])
                if l0_lvl >= 6:
                    ln_stats(y_t, "y", 32, ST, STR, scr_f, "scr")
                for rb in range(RB if l0_lvl >= 6 else 0):
                    S("act", lambda a, rb=rb, ST=ST: a.activation(
                        xn_tmp[:, 1024 * rb:1024 * rb + 1024],
                        y_t[:, 1024 * rb:1024 * rb + 1024],
                        AF.Identity, scale=ST[:, 36 + rb:37 + rb],
                        bias=ST[:, 40 + rb:41 + rb]),
                      reads=["y", STR, "xn_tmp"], writes=["xn_tmp"], name="ln1ap")
                for rb in range(RB if l0_lvl >= 6 else 0):
                    S("dve", lambda d, rb=rb: d.tensor_mul(
                        scr_f[:, 1024 * rb:1024 * rb + 1024],
                        xn_tmp[:, 1024 * rb:1024 * rb + 1024], g0g_bc[:]),
                      reads=["xn_tmp", "g0g", "scr"], writes=["scr"])
                    S("dve", lambda d, rb=rb: d.tensor_add(
                        scr_f[:, 1024 * rb:1024 * rb + 1024],
                        scr_f[:, 1024 * rb:1024 * rb + 1024], b0b_bc[:]),
                      reads=["scr", "b0b"], writes=["scr"])
                    S("dve", lambda d, rb=rb: d.tensor_mul(
                        y_t[:, 1024 * rb:1024 * rb + 1024],
                        scr_f[:, 1024 * rb:1024 * rb + 1024],
                        g0_t[:, 1024 * rb:1024 * rb + 1024]),
                      reads=["scr", "g0"], writes=["y"])
                for hb in range(32 if l0_lvl >= 3 else 0):
                    buf = w2sl[hb % 2]
                    sp_dma([(buf[:], dcache["d_w2"].ap()[128 * hb:128 * hb + 128, :])],
                           reads=["d_w2", f"w2sl{hb % 2}"], writes=[f"w2sl{hb % 2}"],
                           name="w2sl")
                    S("pe", lambda t, buf=buf, hb=hb: [
                        t.matmul(bank_ap(2 * tb + fc),
                                 h_fm[:, 512 * hb + 128 * tb:512 * hb + 128 * tb + 128],
                                 buf[:, 512 * fc:512 * fc + 512],
                                 start=(hb == 0), stop=(hb == 31))
                        for tb in range(RB) for fc in range(2)][-1],
                      reads=[f"w2sl{hb % 2}", "h"],
                      writes=[bank_res(g) for g in range(8)], name="z_mm")
                for tb in range(RB if l0_lvl >= 3 else 0):
                    for fc in range(2):
                        g_ = 2 * tb + fc
                        S("dve", lambda d, tb=tb, fc=fc, g_=g_: d.tensor_add(
                            scr_f[:, 1024 * tb + 512 * fc:1024 * tb + 512 * fc + 512],
                            bank_ap(g_), b2_bc[:, 512 * fc:512 * fc + 512]),
                          reads=[bank_res(g_), "b2bc", "scr"], writes=["scr"])
                        S("act", lambda a, tb=tb, fc=fc: a.activation(
                            g_t[:, 1024 * tb + 512 * fc:1024 * tb + 512 * fc + 512],
                            scr_f[:, 1024 * tb + 512 * fc:1024 * tb + 512 * fc + 512],
                            AF.Silu),
                          reads=["scr"], writes=["g", bank_res(g_)])

                if t_idx == 0:
                    DBG("h", h_fm[:, 0:4096], "h")
                    DBG("g", g_t[:], "g")
                if t_idx == 0:
                    DBG("ygt", y_t[:], "y")
                transpose_tile(y_t, yg_fm, "y", "yg")

                for fc in range(2):
                    buf = stream_wt("d_wo", fc, tag="wo")
                    for tb in range(RB):
                        bk = tb % 2
                        S("pe", lambda t, buf=buf, tb=tb, bk=bk: [
                            t.matmul(pb[bk][:],
                                     yg_fm[:, 512 * eb + 128 * tb:512 * eb + 128 * tb + 128],
                                     buf[:, 512 * eb:512 * eb + 512],
                                     start=(eb == 0), stop=(eb == EB - 1))
                            for eb in range(EB)][-1],
                          reads=["yg", f"wbuf{fc % 2}"], writes=[f"pb{bk}"], name="wo_mm")
                        S("dve", lambda d, tb=tb, fc=fc, bk=bk: d.tensor_mul(
                            scr_f[:, 1024 * tb + 512 * fc:1024 * tb + 512 * fc + 512],
                            g_t[:, 1024 * tb + 512 * fc:1024 * tb + 512 * fc + 512],
                            pb[bk][:]),
                          reads=[f"pb{bk}", "g", "scr"], writes=["scr"])
                        S("dve", lambda d, tb=tb, fc=fc, bk=bk: d.tensor_add(
                            x1_t[:, 1024 * tb + 512 * fc:1024 * tb + 512 * fc + 512],
                            x_t[:, 1024 * tb + 512 * fc:1024 * tb + 512 * fc + 512],
                            scr_f[:, 1024 * tb + 512 * fc:1024 * tb + 512 * fc + 512]),
                          reads=["scr", "x"], writes=["x1", f"pb{bk}"])
            else:
                for rb in range(RB):
                    S("dve", lambda d, rb=rb: d.tensor_copy(
                        x1_t[:, 1024 * rb:1024 * rb + 1024],
                        x_t[:, 1024 * rb:1024 * rb + 1024]),
                      reads=["x"], writes=["x1"])

            if t_idx + 1 < n_tiles:
                nb, nch_ = divmod(t_idx + 1, NCH)
                nrow0 = nb * L + nch_ * 512
                sp_dma([(x_t[:].rearrange("p (a f) -> p a f", f=D),
                         x_in.ap()[nrow0:nrow0 + 512, :].rearrange("(a p) f -> p a f", p=128))],
                       reads=["x"], writes=["x"], name="xpref")

            if 1 not in layers and t_idx + 1 < n_tiles and 0 in layers and l0_lvl >= 1:
                l0_prolog(t_idx + 1)

            if 1 in layers:
                ln_stats(x1_t, "x1", 0, ST, STR, scr_f, "scr")
                for rb in range(RB):
                    S("act", lambda a, rb=rb, ST=ST: a.activation(
                        y_t[:, 1024 * rb:1024 * rb + 1024],
                        x1_t[:, 1024 * rb:1024 * rb + 1024],
                        AF.Identity, scale=ST[:, 4 + rb:5 + rb],
                        bias=ST[:, 8 + rb:9 + rb]),
                      reads=["x1", STR, "y"], writes=["y"], name="ln2ap")
                transpose_tile(y_t, xn1_fm, "y", "xn1_fm")

                for fc in range(4):
                    buf = stream_wt("d_wqkv", fc, tag="qk")
                    for fb in range(4):
                        fglob = 512 * fc + 128 * fb
                        bk = fb % 2
                        S("pe", lambda t, buf=buf, fb=fb, bk=bk: [
                            t.matmul(pb[bk][:],
                                     buf[:, 512 * eb + 128 * fb:512 * eb + 128 * fb + 128],
                                     xn1_fm[:, 512 * eb:512 * eb + 512],
                                     start=(eb == 0), stop=(eb == EB - 1))
                            for eb in range(EB)][-1],
                          reads=[f"wbuf{fc % 2}", "xn1_fm"], writes=[f"pb{bk}"],
                          name="qk_mm")
                        if fglob < 1024:
                            ebq = fglob // 128
                            S("act", lambda a, ebq=ebq, bk=bk: a.activation(
                                yg_fm[:, 512 * ebq:512 * ebq + 512], pb[bk][:],
                                AF.Identity, scale=0.125),
                              reads=[f"pb{bk}"], writes=["q1", f"pb{bk}"])
                        else:
                            ebk = (fglob - 1024) // 128
                            S("act", lambda a, ebk=ebk, bk=bk: a.activation(
                                v_t[:, 512 * ebk:512 * ebk + 512], pb[bk][:], AF.Copy),
                              reads=[f"pb{bk}"], writes=["k1", f"pb{bk}"])
                for fc in range(2):
                    buf = stream_wt("d_wqkv", 4 + fc, tag="v1")
                    for tb in range(RB):
                        bk = tb % 2
                        S("pe", lambda t, buf=buf, tb=tb, bk=bk: [
                            t.matmul(pb[bk][:],
                                     xn1_fm[:, 512 * eb + 128 * tb:512 * eb + 128 * tb + 128],
                                     buf[:, 512 * eb:512 * eb + 512],
                                     start=(eb == 0), stop=(eb == EB - 1))
                            for eb in range(EB)][-1],
                          reads=["xn1_fm", f"wbuf{fc % 2}"], writes=[f"pb{bk}"],
                          name="v1_mm")
                        S("act", lambda a, tb=tb, fc=fc, bk=bk: a.activation(
                            q_t[:, 1024 * tb + 512 * fc:1024 * tb + 512 * fc + 512],
                            pb[bk][:], AF.Copy),
                          reads=[f"pb{bk}"], writes=["v1", f"pb{bk}"])


                def _n(e, ins):
                    e.nop()
                    return ins
                S("dve", lambda d, ST=ST: _n(d, d.tensor_mul(ST[:, 48:52], ST[:, 20:24],
                                                             ST[:, 4:8])),
                  reads=[STR], writes=[STR], small=True)
                S("dve", lambda d, ST=ST: _n(d, d.tensor_mul(ST[:, 60:64], ST[:, 48:52],
                                                             ST[:, 4:8])),
                  reads=[STR], writes=[STR], small=True)
                S("act", lambda a, ST=ST: _n(a, a.activation(ST[:, 52:56], ST[:, 60:64],
                                                             AF.Sqrt, bias=EPS)),
                  reads=[STR], writes=[STR], small=True)
                S("dve", lambda d, ST=ST: _n(d, d.reciprocal(ST[:, 56:60], ST[:, 52:56])),
                  reads=[STR], writes=[STR], small=True)
                S("pe", lambda t, ST=ST: [
                    t.transpose(pb[1][0:1, 128 * rb:128 * rb + 128],
                                ST[:, 56 + rb:57 + rb], eye_f[:])
                    for rb in range(RB)][-1],
                  reads=[STR, "eye", "pb1"], writes=["pb1"], name="r3tr")
                S("act", lambda a: a.activation(rd_row[0][:], pb[1][0:1, 0:512], AF.Copy),
                  reads=["pb1"], writes=["rr0", "pb1"], name="r3ev", small=True)
                S("pe", lambda t: t.matmul(pb[1][:], ones_bf[:], rd_row[0][:],
                                           start=True, stop=True),
                  reads=["rr0", "pb1"], writes=["pb1"], name="r3bc")
                for eb in range(EB):
                    S("dve", lambda d, eb=eb: d.tensor_mul(
                        scr_f[:, 512 * eb:512 * eb + 512],
                        ln3_fm[:, 512 * eb:512 * eb + 512], pb[1][:]),
                      reads=["xn1_fm", "pb1", "scr"], writes=["scr"], name="l3mul")
                for eb in range(EB):
                    S("act", lambda a, eb=eb: a.activation(
                        ln3_fm[:, 512 * eb:512 * eb + 512],
                        scr_f[:, 512 * eb:512 * eb + 512],
                        AF.Identity, scale=g1_col[:, eb:eb + 1], bias=bt1_col[:, eb:eb + 1]),
                      reads=["scr", "g1c", "bt1c"], writes=["xn1_fm"], name="l3ap")

                for j2 in range(16):
                    buf = w1sl[j2 % 2]
                    sp_dma([(buf[:].rearrange("p (a f) -> p a f", f=256),
                             dcache["d_w1d"].ap()[:, 256 * j2:256 * j2 + 256]
                             .rearrange("(a p) f -> p a f", p=128))],
                           reads=["d_w1d", f"w1sl{j2 % 2}"], writes=[f"w1sl{j2 % 2}"],
                           name="w1dsl")
                    for k in range(2):
                        hb = 2 * j2 + k
                        bk = hb % 2
                        S("pe", lambda t, buf=buf, k=k, bk=bk: [
                            t.matmul(pb[bk][:],
                                     buf[:, 256 * eb + 128 * k:256 * eb + 128 * k + 128],
                                     ln3_fm[:, 512 * eb:512 * eb + 512],
                                     start=(eb == 0), stop=(eb == EB - 1))
                            for eb in range(EB)][-1],
                          reads=[f"w1sl{j2 % 2}", "xn1_fm"], writes=[f"pb{bk}"], name="u_mm")
                        S("act", lambda a, hb=hb, bk=bk: a.activation(
                            h_fm[:, 512 * hb:512 * hb + 512], pb[bk][:], AF.Silu,
                            bias=fc1b_col[:, hb:hb + 1]),
                          reads=[f"pb{bk}", "fc1c"], writes=["h", f"pb{bk}"], name="uev")
                for hb in range(32):
                    buf = w2sl[hb % 2]
                    sp_dma([(buf[:], dcache["d_w2d"].ap()[128 * hb:128 * hb + 128, :])],
                           reads=["d_w2d", f"w2sl{hb % 2}"], writes=[f"w2sl{hb % 2}"],
                           name="w2dsl")
                    S("pe", lambda t, buf=buf, hb=hb: [
                        t.matmul(bank_ap(2 * tb + fc),
                                 h_fm[:, 512 * hb + 128 * tb:512 * hb + 128 * tb + 128],
                                 buf[:, 512 * fc:512 * fc + 512],
                                 start=(hb == 0), stop=(hb == 31))
                        for tb in range(RB) for fc in range(2)][-1],
                      reads=[f"w2sl{hb % 2}", "h"],
                      writes=[bank_res(g) for g in range(8)], name="z2_mm")
                for tb in range(RB):
                    for fc in range(2):
                        g_ = 2 * tb + fc
                        S("dve", lambda d, tb=tb, fc=fc, g_=g_: d.scalar_tensor_tensor(
                            scr_f[:, 1024 * tb + 512 * fc:1024 * tb + 512 * fc + 512],
                            bank_ap(g_), 0.5, hfc2_bc[:, 512 * fc:512 * fc + 512],
                            ALU.mult, ALU.add),
                          reads=[bank_res(g_), "hfc2", "scr"], writes=["scr"])
                        S("dve", lambda d, tb=tb, fc=fc: d.tensor_add(
                            scr_f[:, 1024 * tb + 512 * fc:1024 * tb + 512 * fc + 512],
                            scr_f[:, 1024 * tb + 512 * fc:1024 * tb + 512 * fc + 512],
                            y_t[:, 1024 * tb + 512 * fc:1024 * tb + 512 * fc + 512]),
                          reads=["scr", "y"], writes=["scr"])
                        S("act", lambda a, tb=tb, fc=fc, g_=g_: a.activation(
                            g_t[:, 1024 * tb + 512 * fc:1024 * tb + 512 * fc + 512],
                            scr_f[:, 1024 * tb + 512 * fc:1024 * tb + 512 * fc + 512],
                            AF.Silu),
                          reads=["scr"], writes=["g", bank_res(g_)])

                if t_idx + 1 < n_tiles and 0 in layers and l0_lvl >= 1:
                    l0_prolog(t_idx + 1)

                def att_st(h):
                    ebh = h // 2
                    base = 64 * (h % 2)
                    par = h % 4
                    for kb in range(RB):
                        bk = (2 * (h % 2)) + (kb % 2)
                        S("pe", lambda t, h=h, kb=kb, bk=bk, ebh=ebh, base=base: t.matmul(
                            pb[bk][:],
                            v_t[base:base + 64,
                                512 * ebh + 128 * kb:512 * ebh + 128 * kb + 128],
                            yg_fm[base:base + 64, 512 * ebh:512 * ebh + 512],
                            start=True, stop=True),
                          reads=["k1", "q1", f"pb{bk}"], writes=[f"pb{bk}"], name="st_mm")
                        lam_col = 16 * (4 * ch + kb) + h
                        S("act", lambda a, kb=kb, par=par, bk=bk, lam_col=lam_col:
                          a.activation(
                              h_fm[:, 2048 * par + 512 * kb:2048 * par + 512 * kb + 512],
                              pb[bk][:], AF.Exp, scale=lamT[:, lam_col:lam_col + 1]),
                          reads=[f"pb{bk}", "lamT", f"at{par}"],
                          writes=[f"at{par}", f"pb{bk}"], name="expev")

                def att_dv(h):
                    par = h % 4
                    base = 64 * (h % 2)
                    S("pe", lambda t, h=h, par=par: [
                        t.matmul(pb[4][0:16, :], e16_sb[:, 16 * h:16 * h + 16],
                                 h_fm[:, 2048 * par + 512 * kb:2048 * par + 512 * kb + 512],
                                 start=(h == 0 and kb == 0),
                                 stop=(h == NH - 1 and kb == RB - 1))
                        for kb in range(RB)][-1],
                      reads=[f"at{par}", "e16", "pb4"], writes=["pb4"], name="dn_mm")
                    bk5 = 5 + (h % 4) // 2
                    S("pe", lambda t, h=h, par=par, base=base, bk5=bk5: [
                        t.matmul(pb[bk5][base:base + 64, :],
                                 q_t[:, 1024 * kb + 64 * h:1024 * kb + 64 * h + 64],
                                 h_fm[:, 2048 * par + 512 * kb:2048 * par + 512 * kb + 512],
                                 start=(kb == 0), stop=(kb == RB - 1),
                                 tile_position=(0, base) if base else None)
                        for kb in range(RB)][-1],
                      reads=[f"at{par}", "v1", f"pb{bk5}"], writes=[f"pb{bk5}"],
                      name="av_mm")
                    if h % 4 == 3:
                        ebp = h // 4
                        S("dve", lambda d, ebp=ebp: [
                            d.tensor_copy(
                                xn_fm[:, 512 * (2 * ebp + i):512 * (2 * ebp + i) + 512],
                                pb[5 + i][:]) for i in range(2)][-1],
                          reads=["pb5", "pb6"], writes=["att", "pb5", "pb6"],
                          name="attev")

                att_st(0)
                att_st(1)
                for h in range(NH):
                    if h + 2 < NH:
                        att_st(h + 2)
                    att_dv(h)
                S("dve", lambda d: d.reciprocal(scr_f[0:16, 0:512], pb[4][0:16, :]),
                  reads=["pb4", "scr"], writes=["scr", "pb4"])
                S("act", lambda a: a.activation(dn_sb[:], scr_f[0:16, 0:512], AF.Copy),
                  reads=["scr"], writes=["dn"], small=True)
                pl_dma([(dcache["d_dn"].ap(), dn_sb[:])], reads=["dn"], writes=["d_dn"])
                for h in range(NH):
                    rr = rd_row[h % 2]
                    pl_dma([(rr[:], dcache["d_dn"].ap()[h:h + 1, :])],
                           reads=["d_dn", f"rr{h % 2}"], writes=[f"rr{h % 2}"])
                    base = 64 * (h % 2)
                    S("pe", lambda t, rr=rr, base=base: t.matmul(
                        pb[4][base:base + 64, :], ones_bf[0:1, 0:64], rr[:],
                        start=True, stop=True,
                        tile_position=(0, base) if base else None),
                      reads=[f"rr{h % 2}", "pb4"], writes=["pb4"], name="rd_mm")
                    if h % 2 == 1:
                        ebh2 = h // 2
                        S("dve", lambda d, ebh2=ebh2: d.tensor_mul(
                            xn_fm[:, 512 * ebh2:512 * ebh2 + 512],
                            xn_fm[:, 512 * ebh2:512 * ebh2 + 512], pb[4][:]),
                          reads=["att", "pb4"], writes=["att", "pb4"])
                for fc in range(2):
                    buf = stream_wt("d_wout", fc, tag="wop")
                    for tb in range(RB):
                        bk = tb % 2
                        S("pe", lambda t, buf=buf, tb=tb, bk=bk: [
                            t.matmul(pb[bk][:],
                                     xn_fm[:, 512 * eb + 128 * tb:512 * eb + 128 * tb + 128],
                                     buf[:, 512 * eb:512 * eb + 512],
                                     start=(eb == 0), stop=(eb == EB - 1))
                            for eb in range(EB)][-1],
                          reads=["att", f"wbuf{fc % 2}"], writes=[f"pb{bk}"],
                          name="wout_mm")
                        S("dve", lambda d, tb=tb, fc=fc, bk=bk: d.tensor_add(
                            scr_f[:, 1024 * tb + 512 * fc:1024 * tb + 512 * fc + 512],
                            pb[bk][:], bout_bc[:, 512 * fc:512 * fc + 512]),
                          reads=[f"pb{bk}", "boutb", "scr"], writes=["scr"])
                        S("dve", lambda d, tb=tb, fc=fc: d.tensor_mul(
                            scr_f[:, 1024 * tb + 512 * fc:1024 * tb + 512 * fc + 512],
                            scr_f[:, 1024 * tb + 512 * fc:1024 * tb + 512 * fc + 512],
                            g_t[:, 1024 * tb + 512 * fc:1024 * tb + 512 * fc + 512]),
                          reads=["scr", "g"], writes=["scr"])
                        S("dve", lambda d, tb=tb, fc=fc, bk=bk: d.tensor_add(
                            x1_t[:, 1024 * tb + 512 * fc:1024 * tb + 512 * fc + 512],
                            x1_t[:, 1024 * tb + 512 * fc:1024 * tb + 512 * fc + 512],
                            scr_f[:, 1024 * tb + 512 * fc:1024 * tb + 512 * fc + 512]),
                          reads=["scr", "x1"], writes=["x1", f"pb{bk}"])

            pl_dma([(x_out.ap()[row0:row0 + 512, :].rearrange("(a p) f -> p a f", p=128),
                     x1_t[:].rearrange("p (a f) -> p a f", f=D))],
                   reads=["x1"], writes=["xoutd"], name="out")

    for q, host in (("sp", "sp"), ("pl", "pl"), ("aq", "act")):
        fins = P.queue_fin(q)

        def mkfin(fins):
            def f(obj):
                for sem, val in fins:
                    obj.wait_ge(sem, val)
            return f
        P.steps[host].append(("q", mkfin(fins), {}, "fin", None))

    with nc.Block() as blk:
        @blk.sync
        def _(sp):
            P.emit_engine("sp", sp)

        @blk.gpsimd
        def _(g):
            P.emit_engine("pl", g)

        @blk.tensor
        def _(t):
            P.emit_engine("pe", t)

        @blk.scalar
        def _(a):
            P.emit_engine("act", a)

        @blk.vector
        def _(d):
            P.emit_engine("dve", d)

    return nc


_NC_CACHE = {}


def kernel(**inputs):
    layers = (0, 1)
    n_batch = 2
    L = 4096
    key = (layers, n_batch, L)
    if key not in _NC_CACHE:
        _NC_CACHE[key] = build_program(layers, n_batch, L)
    nc = _NC_CACHE[key]

    x = np.ascontiguousarray(inputs["L0_attn_x"])
    in_maps = []
    for c in range(N_CORES):
        m = {"x_in": x[n_batch * c:n_batch * (c + 1)].reshape(n_batch * L, D)}
        for k, v in inputs.items():
            if k != "L0_attn_x":
                m[k] = np.asarray(v)
        in_maps.append(m)
    res = run_bass_kernel_spmd(nc, in_maps, core_ids=list(range(N_CORES)))
    out = np.concatenate([r["x_out"].reshape(n_batch, L, D) for r in res.results],
                         axis=0)
    return out.astype(np.float32, copy=False)


# version 09
# speedup vs baseline: 1.0170x; 1.0014x over previous
import numpy as np
import ml_dtypes

import concourse.bass as bass
import concourse.mybir as mybir
from concourse.bass_utils import run_bass_kernel_spmd

F32 = mybir.dt.float32
BF16 = mybir.dt.bfloat16
I32 = mybir.dt.int32
AF = mybir.ActivationFunctionType
ALU = mybir.AluOpType
AX = mybir.AxisListType

D = 1024
NH = 16
DH = 64
FF = 4096
EPS = 1e-5
N_CORES = 8
RB = 4
EB = 8

BF16_NP = ml_dtypes.bfloat16


class Prog:

    ENGS = ("sp", "pl", "aq", "pe", "act", "dve")
    QUEUES = ("sp", "pl", "aq")
    W = 8

    def __init__(self, nc):
        self.nc = nc
        self.sems = {e: nc.alloc_semaphore(f"s_{e}") for e in ("pe", "act", "dve")}
        self.qsems = {q: [nc.alloc_semaphore(f"s_{q}{i}") for i in range(self.W)]
                      for q in self.QUEUES}
        self.steps = {e: [] for e in self.ENGS}
        self.cum = {e: 0 for e in ("pe", "act", "dve")}
        self.qcum = {q: [0] * self.W for q in self.QUEUES}
        self.qidx = {q: 0 for q in self.QUEUES}
        self.res = {}
        self.small_sids = set()

    def step(self, eng, emit, reads=(), writes=(), name="", small=False):
        deps = {}

        def _add(sid):
            if sid is None:
                return
            if sid[0] == "c":
                if sid[1] != eng or sid in self.small_sids:
                    key = ("c", sid[1])
                    deps[key] = max(deps.get(key, 0), sid[2])
            else:
                if sid[1] != eng:
                    key = ("q", sid[1], sid[2])
                    deps[key] = max(deps.get(key, 0), sid[3])

        for r in reads:
            ent = self.res.get(r)
            if ent:
                _add(ent[0])
        for r in writes:
            ent = self.res.get(r)
            if ent:
                _add(ent[0])
                for rd in ent[1]:
                    _add(rd)
        self.cum[eng] += 1
        sid = ("c", eng, self.cum[eng])
        if small:
            self.small_sids.add(sid)
        self.steps[eng].append(("c", emit, deps, name, None))
        self._commit(sid, reads, writes)
        return sid

    def dma(self, q, outs_ins, reads=(), writes=(), name="", transpose=False):
        deps = {}

        def _add(sid):
            if sid is None:
                return
            if sid[0] == "c":
                key = ("c", sid[1])
                deps[key] = max(deps.get(key, 0), sid[2])
            else:
                key = ("q", sid[1], sid[2])
                deps[key] = max(deps.get(key, 0), sid[3])

        for r in reads:
            ent = self.res.get(r)
            if ent:
                _add(ent[0])
        for r in writes:
            ent = self.res.get(r)
            if ent:
                _add(ent[0])
                for rd in ent[1]:
                    _add(rd)
        slot = self.qidx[q] % self.W
        self.qidx[q] += 1
        before = self.qcum[q][slot]
        self.qcum[q][slot] += 16 * len(outs_ins)
        sid = ("q", q, slot, self.qcum[q][slot])
        sem = self.qsems[q][slot]

        def emit(obj, outs_ins=outs_ins, sem=sem, transpose=transpose):
            for o, i in outs_ins:
                obj.dma_start(o, i, transpose=transpose).then_inc(sem, 16)

        host = "act" if q == "aq" else q
        self.steps[host].append(("q", emit, deps, name, (q, slot, before)))
        self._commit(sid, reads, writes)
        return sid

    def _commit(self, sid, reads, writes):
        for r in writes:
            self.res[r] = [sid, []]
        for r in reads:
            ent = self.res.setdefault(r, [None, []])
            ent[1].append(sid)

    def emit_engine(self, eng, obj):
        waited = {}
        for kind, emit, deps, name, qinfo in self.steps[eng]:
            if qinfo is not None:
                qq, slot, before = qinfo
                if before > 0:
                    obj.wait_ge(self.qsems[qq][slot], before)
            for key, dv in sorted(deps.items(), key=str):
                if dv > waited.get(key, 0):
                    sem = (self.sems[key[1]] if key[0] == "c"
                           else self.qsems[key[1]][key[2]])
                    obj.wait_ge(sem, dv)
                    waited[key] = dv
            last = emit(obj)
            if eng in ("pe", "act", "dve") and last is not None:
                last.then_inc(self.sems[eng], 1)

    def queue_fin(self, q):
        return [(self.qsems[q][s], self.qcum[q][s]) for s in range(self.W)
                if self.qcum[q][s] > 0]


def _log_pos_const(L):
    lp = np.zeros((2, L), np.float32)
    lp[0] = np.log(np.arange(L, dtype=np.float64) + 1.0).astype(np.float32)
    lp[1] = 1.0
    return lp


def build_program(layers=(0, 1), n_batch=2, L=4096, detect_races=False, dbg=False, l0_lvl=99, ph0=True, setup=True):
    NCH = L // 512
    n_tiles = n_batch * NCH
    nc = bass.Bass("TRN2", target_bir_lowering=False,
                   detect_race_conditions=detect_races)

    x_in = nc.dram_tensor("x_in", [n_batch * L, D], F32, kind="ExternalInput")
    x_out = nc.dram_tensor("x_out", [n_batch * L, D], F32, kind="ExternalOutput")

    win = {}
    def WIN(name, shape, dtype=F32):
        win[name] = nc.dram_tensor(name, shape, dtype, kind="ExternalInput")

    if 0 in layers:
        WIN("L0_attn_Wq", [D, D]); WIN("L0_attn_Wv", [D, D])
        WIN("L0_attn_Wg", [D, D]); WIN("L0_attn_Wo", [D, D])
        WIN("L0_attn_beta", [NH]); WIN("L0_attn_ln_gamma", [D]); WIN("L0_attn_ln_beta", [D])
        WIN("L0_ffn_w1", [D, FF]); WIN("L0_ffn_b1", [FF])
        WIN("L0_ffn_w2", [FF, D]); WIN("L0_ffn_b2", [D])
    if 1 in layers:
        WIN("L1_attn_Wqkv", [3 * D, D]); WIN("L1_attn_Wout", [D, D]); WIN("L1_attn_bout", [D])
        WIN("L1_attn_Wl1", [64, 1]); WIN("L1_attn_bl1", [64])
        WIN("L1_attn_Wl2", [NH, 64]); WIN("L1_attn_bl2", [NH])
        WIN("L1_ffn_ln_gamma", [D]); WIN("L1_ffn_ln_beta", [D])
        WIN("L1_ffn_fc1_q", [D, FF], I32); WIN("L1_ffn_fc1_scales", [8, FF])
        WIN("L1_ffn_fc1_bias", [FF])
        WIN("L1_ffn_fc2_q", [FF, D], I32); WIN("L1_ffn_fc2_scales", [32, D])
        WIN("L1_ffn_fc2_bias", [D])

    dbg_out = {}
    def DBG(name, buf_ap, res, cols=4096, dt=BF16):
        if not dbg:
            return
        t = nc.dram_tensor(f"dbg_{name}", [128, cols], dt, kind="ExternalOutput")
        dbg_out[name] = t
        pl_dma_ref[0]([(t.ap(), buf_ap)], reads=[res], writes=[f"dbgd_{name}"], name=f"dbg{name}")

    pl_dma_ref = [None]
    dcache = {}
    def DC(name, shape, dtype=BF16):
        dcache[name] = nc.dram_tensor(name, shape, dtype, kind="Internal")

    if 0 in layers:
        DC("d_wq", [D, D]); DC("d_wv", [D, D]); DC("d_wg", [D, D]); DC("d_wo", [D, D])
        DC("d_wqT", [D, D]); DC("d_wvT", [D, D]); DC("d_wgT", [D, D]); DC("d_woT", [D, D])
        DC("d_w1", [D, FF]); DC("d_w2", [FF, D])
    if 1 in layers:
        DC("d_wqkv", [3 * D, D]); DC("d_wout", [D, D])
        DC("d_wqkvT", [D, 3 * D]); DC("d_woutT", [D, D])
        DC("d_w1d", [D, FF]); DC("d_w2d", [FF, D])
        DC("d_dn", [NH, 512], BF16)

    eye_bf_d = nc.inline_tensor(np.eye(128).astype(BF16_NP), "c_eye_bf")
    eye_f_d = nc.inline_tensor(np.eye(128, dtype=np.float32), "c_eye_f")
    if 0 in layers:
        jj, ii = np.mgrid[0:128, 0:128]
        delta_d = nc.inline_tensor((ii - jj).astype(np.float32), "c_delta")
        tri01_d = nc.inline_tensor((ii >= jj).astype(np.float32), "c_tri01")
        trineg_d = nc.inline_tensor(
            np.where(ii >= jj, 0.0, -30000.0).astype(np.float32), "c_trineg")
    if 1 in layers:
        lp_d = nc.inline_tensor(_log_pos_const(L), "c_logpos")
        e16 = np.zeros((NH, 128, NH), np.float32)
        for h in range(NH):
            e16[h, :, h] = 1.0
        e16_d = nc.inline_tensor(e16.reshape(NH * 128, NH).astype(BF16_NP), "c_e16")

    SB = nc.alloc_sbuf_tensor
    eye_bf = SB("eye_bf", [128, 128], BF16)
    eye_f = SB("eye_f", [128, 128], F32)
    ones_bf = SB("ones_bf", [1, 128], BF16)
    ones_f = SB("ones_f", [1, 128], F32)
    cst_eps = SB("cst_eps", [128, 1], F32)
    stats = SB("stats", [128, 64], F32)
    stats_b = SB("stats_b", [128, 64], F32)
    x_t = SB("x_t", [128, 4 * D], F32)
    x1_t = SB("x1_t", [128, 4 * D], F32)
    scr_f = SB("scr_f", [128, 4 * D], F32)
    xn_tmp = SB("xn_tmp", [128, 4 * D], BF16)
    xn_fm = SB("xn_fm", [128, 8 * 512], BF16)
    h_fm = SB("h_fm", [128, 32 * 512], BF16)
    g_t = SB("g_t", [128, 4 * D], BF16)
    q_t = SB("q_t", [128, 4 * D], BF16)
    v_t = SB("v_t", [128, 4 * D], BF16)
    y_t = SB("y_t", [128, 4 * D], BF16)
    g0_t = SB("g0_t", [128, 4 * D], BF16)
    yg_fm = SB("yg_fm", [128, 8 * 512], BF16)
    xn1_fm = SB("xn1_fm", [128, 8 * 512], BF16)
    ln3_fm = None
    ln3_fm = xn1_fm
    wbuf = [SB(f"wbuf{i}", [128, 8 * 512], BF16) for i in range(2)]
    w1sl = [SB(f"w1sl{i}", [128, 2048], BF16) for i in range(2)]
    w2sl = [SB(f"w2sl{i}", [128, 1024], BF16) for i in range(2)]
    b2_bc = SB("b2_bc", [128, D], BF16)
    bout_bc = SB("bout_bc", [128, D], BF16)
    g0g_bc = SB("g0g_bc", [128, D], BF16)
    b0b_bc = SB("b0b_bc", [128, D], BF16)
    hfc2_bc = SB("hfc2_bc", [128, D], BF16)
    b1_col = SB("b1_col", [128, 32], F32)
    fc1b_col = SB("fc1b_col", [128, 32], F32)
    g1_col = SB("g1_col", [128, 8], F32)
    bt1_col = SB("bt1_col", [128, 8], F32)
    lt_all = SB("lt_all", [128, NH * 128], BF16)
    lam_row = SB("lam_row", [1, NH], F32)
    lnlam_cols = SB("lnlam_cols", [128, NH], F32)
    carry_sb = SB("carry_sb", [1, D], BF16)
    lamT = SB("lamT", [128, (L // 128) * NH], F32)
    wl1b = SB("wl1b", [2, 64], F32)
    wl2t = SB("wl2t", [64, 16], F32)
    wl2_sb = SB("wl2_sb", [16, 64], F32)
    bl2c = SB("bl2c", [16, 1], F32)
    e16_sb = SB("e16_sb", [128, 16 * NH], BF16)
    dn_sb = SB("dn_sb", [16, 512], BF16)
    rd_row = [SB(f"rd_row{i}", [1, 512], BF16) for i in range(2)]
    scrow = [SB(f"scrow{i}", [1, 512], BF16) for i in range(2)]
    misc_r = SB("misc_r", [1, 64], F32)
    delta_sb = x1_t[:, 0:128]
    tri01_sb = x1_t[:, 128:256]
    trineg_sb = x1_t[:, 256:384]

    PS = nc.alloc_psum_tensor
    pb = [PS(f"pb{i}", [128, 512], F32) for i in range(7)]
    pbt = PS("pbt", [128, 1024], BF16)
    carry_ps = pbt[0:1, :].bitcast(F32)
    pbt_f32 = pbt[:, :].bitcast(F32)

    def bank_ap(g):
        return pb[g][:] if g < 7 else pbt_f32

    def bank_res(g):
        return f"pb{g}" if g < 7 else "pbt"

    with nc.Block() as blk0:
        @blk0.gpsimd
        def _(g):
            g.memset(cst_eps[:], EPS)
            g.memset(ones_bf[:], 1.0)
            g.memset(ones_f[:], 1.0)
            g.memset(carry_sb[:], 0.0)
    nc.const_aps.register(F32, EPS, cst_eps[:])

    P = Prog(nc)
    S = P.step

    def sp_dma(outs_ins, reads, writes, name=""):
        return P.dma("sp", outs_ins, reads, writes, name=name)

    XBAR_Q = "aq"

    def sp_dma_t(outs_ins, reads, writes, name=""):
        return P.dma(XBAR_Q, outs_ins, reads, writes, name=name, transpose=True)

    def pl_dma(outs_ins, reads, writes, name=""):
        return P.dma("pl", outs_ins, reads, writes, name=name)
    pl_dma_ref[0] = pl_dma

    def ln_stats(src_tile, src_res, c, ST, STR, sq, sq_res):
        def _n(d, ins):
            d.nop()
            return ins
        for rb in range(RB):
            S("dve", lambda d, rb=rb: d.tensor_mul(
                sq[:, 1024 * rb:1024 * rb + 1024],
                src_tile[:, 1024 * rb:1024 * rb + 1024],
                src_tile[:, 1024 * rb:1024 * rb + 1024]),
              reads=[src_res, sq_res], writes=[sq_res])
        for rb in range(RB):
            S("dve", lambda d, rb=rb: _n(d, d.reduce_sum(
                ST[:, c + 16 + rb:c + 17 + rb],
                src_tile[:, 1024 * rb:1024 * rb + 1024], axis=AX.X)),
              reads=[src_res], writes=[STR], small=True)
            S("dve", lambda d, rb=rb: _n(d, d.reduce_sum(
                ST[:, c + 20 + rb:c + 21 + rb],
                sq[:, 1024 * rb:1024 * rb + 1024], axis=AX.X)),
              reads=[sq_res], writes=[STR], small=True)
        S("dve", lambda d: _n(d, d.tensor_scalar_mul(ST[:, c:c + 4],
                                                     ST[:, c + 16:c + 20], 1.0 / D)),
          reads=[STR], writes=[STR], small=True)
        S("dve", lambda d: _n(d, d.tensor_scalar_mul(ST[:, c + 12:c + 16],
                                                     ST[:, c + 20:c + 24], 1.0 / D)),
          reads=[STR], writes=[STR], small=True)
        S("dve", lambda d: _n(d, d.tensor_mul(ST[:, c + 24:c + 28], ST[:, c:c + 4],
                                              ST[:, c:c + 4])),
          reads=[STR], writes=[STR], small=True)
        S("dve", lambda d: _n(d, d.tensor_sub(ST[:, c + 20:c + 24],
                                              ST[:, c + 12:c + 16],
                                              ST[:, c + 24:c + 28])),
          reads=[STR], writes=[STR], small=True)
        S("act", lambda a: _n(a, a.activation(ST[:, c + 24:c + 28],
                                              ST[:, c + 20:c + 24],
                                              AF.Sqrt, bias=EPS)),
          reads=[STR], writes=[STR], small=True)
        S("dve", lambda d: _n(d, d.reciprocal(ST[:, c + 4:c + 8],
                                              ST[:, c + 24:c + 28])),
          reads=[STR], writes=[STR], small=True)
        S("dve", lambda d: _n(d, d.tensor_mul(ST[:, c + 8:c + 12], ST[:, c:c + 4],
                                              ST[:, c + 4:c + 8])),
          reads=[STR], writes=[STR], small=True)
        S("dve", lambda d: _n(d, d.tensor_scalar_mul(ST[:, c + 8:c + 12],
                                                     ST[:, c + 8:c + 12], -1.0)),
          reads=[STR], writes=[STR], small=True)

    h_f32 = h_fm[:].bitcast(F32)

    def stats_of(t):
        return (stats if t % 2 == 0 else stats_b), f"st{t % 2}"

    def l0_prolog(t, sq=None, sq_res="h"):
        ST, STR = stats_of(t)
        ln_stats(x_t, "x", 0, ST, STR,
                 sq if sq is not None else h_f32[:, 0:4096], sq_res)
        for rb in range(RB):
            S("act", lambda a, rb=rb, ST=ST: a.activation(
                xn_tmp[:, 1024 * rb:1024 * rb + 1024],
                x_t[:, 1024 * rb:1024 * rb + 1024],
                AF.Identity, scale=ST[:, 4 + rb:5 + rb],
                bias=ST[:, 8 + rb:9 + rb]),
              reads=["x", STR, "xn_tmp"], writes=["xn_tmp"], name="ln0ap")

    def transpose_tile(src_bf, dst_fm, src_res, dst_res):
        for rb in range(RB):
            S("pe", lambda t, rb=rb: [
                t.transpose(pbt[:, 128 * eb:128 * eb + 128],
                            src_bf[:, 1024 * rb + 128 * eb:1024 * rb + 128 * eb + 128],
                            eye_bf[:]) for eb in range(EB)][-1],
              reads=[src_res, "eye", "pbt"], writes=["pbt"])
            S("dve", lambda d, rb=rb: [
                d.tensor_copy(dst_fm[:, 512 * eb + 128 * rb:512 * eb + 128 * rb + 128],
                              pbt[:, 128 * eb:128 * eb + 128]) for eb in range(EB)][-1],
              reads=["pbt"], writes=[dst_res])

    def stream_wt(dname, fc, tag=""):
        buf = wbuf[fc % 2]
        tsrc = dcache[dname + "T"]
        sp_dma([(buf[:].rearrange("p (a f) -> p a f", f=512),
                 tsrc.ap()[:, 512 * fc:512 * fc + 512]
                 .rearrange("(a p) f -> p a f", p=128))],
               reads=[f"d_{dname[2:]}T", f"wbuf{fc % 2}"],
               writes=[f"wbuf{fc % 2}"], name=tag)
        return buf

    def proj_tm(dname, src_fm, src_res, dst, dst_res, act_fn, act_kw=None, tag=""):
        for fc in range(2):
            buf = stream_wt(dname, fc, tag=tag)
            for tb in range(RB):
                bk = tb % 2
                S("pe", lambda t, buf=buf, tb=tb, bk=bk: [
                    t.matmul(pb[bk][:],
                             src_fm[:, 512 * eb + 128 * tb:512 * eb + 128 * tb + 128],
                             buf[:, 512 * eb:512 * eb + 512],
                             start=(eb == 0), stop=(eb == EB - 1)) for eb in range(EB)][-1],
                  reads=[src_res, f"wbuf{fc % 2}"], writes=[f"pb{bk}"], name=f"{tag}mm")
                kw = dict(act_kw or {})
                S("act", lambda a, dst=dst, tb=tb, fc=fc, bk=bk, kw=kw: a.activation(
                    dst[:, 1024 * tb + 512 * fc:1024 * tb + 512 * fc + 512],
                    pb[bk][:], act_fn, **kw),
                  reads=[f"pb{bk}"] + list((act_kw or {}).pop("_res", [])),
                  writes=[dst_res, f"pb{bk}"], name=f"{tag}ev")


    sp_dma([(x_t[:].rearrange("p (a f) -> p a f", f=D),
             x_in.ap()[0:512, :].rearrange("(a p) f -> p a f", p=128))],
           reads=["x"], writes=["x"], name="xload")
    if 0 in layers:
        l0_prolog(0, sq=scr_f[:, 0:4096], sq_res="scr")

    def cast_weight(src, dst, R, C, tag):
        for si, r0 in enumerate(range(0, R, 128)):
            slot = si % 4
            buf = h_fm[:, slot * 4096:slot * 4096 + C]
            pl_dma([(buf, src.ap()[r0:r0 + 128, :])],
                   reads=[f"cstg{slot}"], writes=[f"cstg{slot}"], name=f"c_{tag}")
            sp_dma([(dst.ap()[r0:r0 + 128, :], buf)],
                   reads=[f"cstg{slot}"], writes=[f"d_{tag}"])

    def build_wt(bname, tname, FT):
        for fc in range(FT // 512):
            for eb in range(EB):
                stg = wbuf[0][:, 512 * (eb % 4):512 * (eb % 4) + 512]
                P.dma("aq", [(stg, dcache[bname].ap()[512 * fc:512 * fc + 512,
                                                      128 * eb:128 * eb + 128])],
                      reads=[f"d_{bname[2:]}", f"wt{eb % 4}"], writes=[f"wt{eb % 4}"],
                      name="wtx", transpose=True)
                sp_dma([(dcache[tname].ap()[128 * eb:128 * eb + 128,
                                            512 * fc:512 * fc + 512], stg)],
                       reads=[f"wt{eb % 4}"], writes=[f"d_{tname[2:]}"], name="wts")

    if 0 in layers and ph0:
        cast_weight(win["L0_ffn_w1"], dcache["d_w1"], D, FF, "w1")
        cast_weight(win["L0_ffn_w2"], dcache["d_w2"], FF, D, "w2")
        cast_weight(win["L0_attn_Wq"], dcache["d_wq"], D, D, "wq")
        cast_weight(win["L0_attn_Wv"], dcache["d_wv"], D, D, "wv")
        cast_weight(win["L0_attn_Wg"], dcache["d_wg"], D, D, "wg")
        build_wt("d_wq", "d_wqT", D)
        build_wt("d_wv", "d_wvT", D)
        build_wt("d_wg", "d_wgT", D)
        cast_weight(win["L0_attn_Wo"], dcache["d_wo"], D, D, "wo")
        build_wt("d_wo", "d_woT", D)
    if 1 in layers and ph0:
        cast_weight(win["L1_attn_Wqkv"], dcache["d_wqkv"], 3 * D, D, "wqkv")
        build_wt("d_wqkv", "d_wqkvT", 3 * D)
        cast_weight(win["L1_attn_Wout"], dcache["d_wout"], D, D, "wout")
        build_wt("d_wout", "d_woutT", D)

        def dequant(qsrc, ssrc, dst, R, C, tag):
            for si, r0 in enumerate(range(0, R, 128)):
                qb = x1_t[:].bitcast(I32)[:, 0:C]
                pl_dma([(qb, qsrc.ap()[r0:r0 + 128, :])],
                       reads=["x1"], writes=["x1"], name=f"dq_{tag}")
                S("dve", lambda d, qb=qb, C=C: d.tensor_copy(scr_f[:, 0:C], qb),
                  reads=["x1"], writes=["scr"])
                for ci, c0 in enumerate(range(0, C, 512)):
                    sr = scrow[ci % 2]
                    pl_dma([(sr[:], ssrc.ap()[si:si + 1, c0:c0 + 512])],
                           reads=[f"sr{ci % 2}"], writes=[f"sr{ci % 2}"])
                    bk = ci % 2
                    S("pe", lambda t, sr=sr, bk=bk: t.matmul(
                        pb[bk][:], ones_bf[:], sr[:], start=True, stop=True),
                      reads=[f"sr{ci % 2}"], writes=[f"pb{bk}"])
                    S("dve", lambda d, c0=c0, bk=bk: d.scalar_tensor_tensor(
                        g0_t[:, c0:c0 + 512], scr_f[:, c0:c0 + 512], -128.0,
                        pb[bk][:], ALU.add, ALU.mult),
                      reads=["scr", f"pb{bk}"], writes=["g0", f"pb{bk}"])
                sp_dma([(dst.ap()[r0:r0 + 128, :], g0_t[:, 0:C])],
                       reads=["g0"], writes=[f"d_{tag}"])

        dequant(win["L1_ffn_fc1_q"], win["L1_ffn_fc1_scales"], dcache["d_w1d"], D, FF, "w1d")
        dequant(win["L1_ffn_fc2_q"], win["L1_ffn_fc2_scales"], dcache["d_w2d"], FF, D, "w2d")

    S("dve", lambda d: d.nop(),
      reads=["wt0", "wt1", "wt2", "wt3", "cstg0", "cstg1", "cstg2", "cstg3"],
      writes=["wbuf0", "wbuf1", "h"], name="wtfence")

    sp_dma([(eye_bf[:], eye_bf_d.ap()), (eye_f[:], eye_f_d.ap())],
           reads=[], writes=["eye"])

    def bc_row(src_ap_row, dst, dst_res, scale=1.0):
        for c0 in range(0, D, 512):
            row = scrow[0]
            pl_dma([(row[:], src_ap_row[0:1, c0:c0 + 512])],
                   reads=["sr0"], writes=["sr0"], name="bcr")
            S("pe", lambda t, row=row: t.matmul(pb[2][:], ones_bf[:], row[:],
                                                start=True, stop=True),
              reads=["sr0"], writes=["pb2"])
            S("act", lambda a, c0=c0, dst=dst, scale=scale: a.activation(
                dst[:, c0:c0 + 512], pb[2][:], AF.Copy, scale=scale),
              reads=["pb2"], writes=[dst_res, "pb2"])

    def col_build(src_ap, dst, n, dst_res):
        sp_dma([(scr_f[0:n, 0:128], src_ap.rearrange("(a p) -> a p", p=128))],
               reads=["scr"], writes=["scr"], name="colb")
        S("pe", lambda t, n=n: t.transpose(pb[3][:, 0:n], scr_f[0:n, 0:128],
                                           eye_f[0:n, 0:n]),
          reads=["scr", "eye"], writes=["pb3"])
        S("act", lambda a, n=n, dst=dst: a.activation(dst[:, 0:n], pb[3][:, 0:n], AF.Copy),
          reads=["pb3"], writes=[dst_res, "pb3"])

    def row1(ap):
        return ap.rearrange("(a b) -> a b", a=1)

    if 0 in layers and setup:
        sp_dma([(delta_sb, delta_d.ap()), (tri01_sb, tri01_d.ap()),
                (trineg_sb, trineg_d.ap()),
                (misc_r[0:1, 0:16], row1(win["L0_attn_beta"].ap()))],
               reads=["x1"], writes=["scanc", "x1"])
        S("act", lambda a: (a.activation(lam_row[:], misc_r[0:1, 0:16], AF.Sigmoid),
                            a.nop())[0],
          reads=["scanc"], writes=["lamr"], small=True)
        S("act", lambda a: (a.activation(misc_r[0:1, 16:32], lam_row[:], AF.Ln),
                            a.nop())[0],
          reads=["lamr", "scanc"], writes=["scanc"], small=True)
        for h in range(NH):
            S("pe", lambda t, h=h: t.matmul(pb[0][:, h:h + 1], ones_f[:],
                                            misc_r[0:1, 16 + h:17 + h],
                                            start=True, stop=True),
              reads=["scanc"], writes=["pb0"], small=True)
        S("act", lambda a: a.activation(lnlam_cols[:], pb[0][:, 0:NH], AF.Copy),
          reads=["pb0"], writes=["lnlamc", "pb0"], small=True)
        for h in range(NH):
            S("act", lambda a, h=h: (a.activation(scr_f[:, 0:128], delta_sb,
                                                  AF.Identity,
                                                  scale=lnlam_cols[:, h:h + 1]),
                                     a.nop())[0],
              reads=["lnlamc", "scanc", "scr", "x1"], writes=["scr"], small=True)
            S("dve", lambda d: (d.tensor_mul(scr_f[:, 128:256], scr_f[:, 0:128],
                                             tri01_sb), d.nop())[0],
              reads=["scr", "scanc", "x1"], writes=["scr"], small=True)
            S("dve", lambda d: (d.tensor_add(scr_f[:, 128:256], scr_f[:, 128:256],
                                             trineg_sb), d.nop())[0],
              reads=["scr", "scanc", "x1"], writes=["scr"], small=True)
            S("act", lambda a, h=h: a.activation(lt_all[:, 128 * h:128 * h + 128],
                                                 scr_f[:, 128:256], AF.Exp),
              reads=["scr"], writes=["lt"], small=True)
        bc_row(row1(win["L0_attn_ln_gamma"].ap()), g0g_bc, "g0g")
        bc_row(row1(win["L0_attn_ln_beta"].ap()), b0b_bc, "b0b")
        bc_row(row1(win["L0_ffn_b2"].ap()), b2_bc, "b2bc")
        col_build(win["L0_ffn_b1"].ap(), b1_col, 32, "b1c")

    if 1 in layers and setup:
        bc_row(row1(win["L1_attn_bout"].ap()), bout_bc, "boutb")
        bc_row(row1(win["L1_ffn_fc2_bias"].ap()), hfc2_bc, "hfc2", scale=0.5)
        col_build(win["L1_ffn_fc1_bias"].ap(), fc1b_col, 32, "fc1c")
        col_build(win["L1_ffn_ln_gamma"].ap(), g1_col, 8, "g1c")
        col_build(win["L1_ffn_ln_beta"].ap(), bt1_col, 8, "bt1c")
        sp_dma([(e16_sb[:].rearrange("p (a f) -> p a f", f=NH),
                 e16_d.ap().rearrange("(a p) f -> p a f", p=128))],
               reads=[], writes=["e16"])
        lp_sb = x1_t
        sp_dma([(wl1b[0:1, :], win["L1_attn_Wl1"].ap().rearrange("a b -> b a")),
                (wl1b[1:2, :], row1(win["L1_attn_bl1"].ap())),
                (bl2c[:], win["L1_attn_Wl2"].ap()[0:0 + NH, 0:1] if False else
                 win["L1_attn_bl2"].ap().rearrange("(p a) -> p a", a=1)),
                (wl2_sb[:], win["L1_attn_Wl2"].ap())],
               reads=[], writes=["lamnc"])
        S("pe", lambda t: t.transpose(pb[4][0:64, 0:16], wl2_sb[:], eye_f[0:16, 0:16]),
          reads=["lamnc", "eye"], writes=["pb4"])
        S("act", lambda a: a.activation(wl2t[:, 0:16], pb[4][0:64, 0:16], AF.Copy),
          reads=["pb4"], writes=["wl2t", "pb4"], small=True)
        for half in range((L + 2047) // 2048):
            c0 = half * 2048
            cw = min(2048, L - c0)
            sp_dma([(lp_sb[0:2, 0:cw], lp_d.ap()[0:2, c0:c0 + cw])],
                   reads=["x1"], writes=["x1"])
            for j in range(cw // 128):
                bk = 4 + (j % 2)
                S("pe", lambda t, j=j, bk=bk: t.matmul(
                    pb[bk][0:64, 0:128], wl1b[:], lp_sb[0:2, 128 * j:128 * j + 128],
                    start=True, stop=True),
                  reads=["lamnc", "x1"], writes=[f"pb{bk}"])
                S("act", lambda a, j=j, c0=c0, bk=bk: a.activation(
                    scr_f[0:64, c0 + 128 * j:c0 + 128 * j + 128],
                    pb[bk][0:64, 0:128], AF.Relu),
                  reads=[f"pb{bk}"], writes=["scr", f"pb{bk}"])
        for j in range(L // 512):
            S("pe", lambda t, j=j: t.matmul(pb[5][0:16, :], wl2t[:, 0:16],
                                            scr_f[0:64, 512 * j:512 * j + 512],
                                            start=True, stop=True),
              reads=["wl2t", "scr"], writes=["pb5"])
            S("act", lambda a, j=j: a.activation(x1_t[0:16, 512 * j:512 * j + 512],
                                                 pb[5][0:16, :], AF.Sigmoid,
                                                 bias=bl2c[0:16, 0:1]),
              reads=["pb5", "lamnc"], writes=["x1", "pb5"])
        for kb in range(L // 128):
            S("pe", lambda t, kb=kb: t.transpose(pb[6][:, 0:16],
                                                 x1_t[0:16, 128 * kb:128 * kb + 128],
                                                 eye_f[0:16, 0:16]),
              reads=["x1", "eye"], writes=["pb6"])
            S("act", lambda a, kb=kb: a.activation(lamT[:, 16 * kb:16 * kb + 16],
                                                   pb[6][:, 0:16], AF.Copy),
              reads=["pb6"], writes=["lamT", "pb6"])

    for b in range(n_batch):
        for ch in range(NCH):
            t_idx = b * NCH + ch
            row0 = b * L + ch * 512
            ST, STR = stats_of(t_idx)

            if 0 in layers and l0_lvl >= 1:
                transpose_tile(xn_tmp, xn_fm, "xn_tmp", "xn_fm")

                for j2 in range(16 if l0_lvl >= 2 else 0):
                    buf = w1sl[j2 % 2]
                    sp_dma([(buf[:].rearrange("p (a f) -> p a f", f=256),
                             dcache["d_w1"].ap()[:, 256 * j2:256 * j2 + 256]
                             .rearrange("(a p) f -> p a f", p=128))],
                           reads=["d_w1", f"w1sl{j2 % 2}"], writes=[f"w1sl{j2 % 2}"],
                           name="w1sl")
                    for k in range(2):
                        hb = 2 * j2 + k
                        bk = hb % 2
                        S("pe", lambda t, buf=buf, k=k, bk=bk: [
                            t.matmul(pb[bk][:],
                                     buf[:, 256 * eb + 128 * k:256 * eb + 128 * k + 128],
                                     xn_fm[:, 512 * eb:512 * eb + 512],
                                     start=(eb == 0), stop=(eb == EB - 1))
                            for eb in range(EB)][-1],
                          reads=[f"w1sl{j2 % 2}", "xn_fm"], writes=[f"pb{bk}"], name="h_mm")
                        S("act", lambda a, hb=hb, bk=bk: a.activation(
                            h_fm[:, 512 * hb:512 * hb + 512], pb[bk][:], AF.Gelu,
                            bias=b1_col[:, hb:hb + 1]),
                          reads=[f"pb{bk}", "b1c"], writes=["h", f"pb{bk}"], name="hev")
                if l0_lvl >= 4:
                    proj_tm("d_wq", xn_fm, "xn_fm", q_t, "q", AF.Copy, tag="q")
                    proj_tm("d_wv", xn_fm, "xn_fm", v_t, "v", AF.Copy, tag="v")
                    proj_tm("d_wg", xn_fm, "xn_fm", g0_t, "g0", AF.Silu, tag="gp")

                if t_idx == 0:
                    DBG("q", q_t[:], "q")
                    DBG("v", v_t[:], "v")
                    DBG("g0", g0_t[:], "g0")
                if ch == 0 and l0_lvl >= 5:
                    S("dve", lambda d: d.memset(carry_sb[:], 0.0),
                      reads=["carrysb"], writes=["carrysb"])
                for half in range(2 if l0_lvl >= 5 else 0):
                    hs = range(8 * half, 8 * half + 8)
                    for rc in range(RB):
                        if rc == 0:
                            S("dve", lambda d, half=half: [
                                d.scalar_tensor_tensor(
                                    v_t[0:1, 64 * h:64 * h + 64],
                                    carry_sb[0:1, 64 * h:64 * h + 64],
                                    lam_row[0:1, h:h + 1],
                                    v_t[0:1, 64 * h:64 * h + 64],
                                    ALU.mult, ALU.add)
                                for h in range(8 * half, 8 * half + 8)][-1],
                              reads=["carrysb", "lamr", "v"], writes=["v"], small=True)
                        else:
                            S("dve", lambda d, half=half, rc=rc: [
                                d.scalar_tensor_tensor(
                                    v_t[0:1, 1024 * rc + 64 * h:1024 * rc + 64 * h + 64],
                                    carry_ps[0:1, 64 * (h % 8):64 * (h % 8) + 64],
                                    lam_row[0:1, h:h + 1],
                                    v_t[0:1, 1024 * rc + 64 * h:1024 * rc + 64 * h + 64],
                                    ALU.mult, ALU.add)
                                for h in range(8 * half, 8 * half + 8)][-1],
                              reads=["pbt", "lamr", "v"], writes=["v"], small=True)
                        sbk = 4 if rc % 2 == 0 else 0
                        S("pe", lambda t, half=half, rc=rc, sbk=sbk: [
                            ins for h in range(8 * half, 8 * half + 8)
                            for ins in (
                                t.matmul(pb[sbk][:, 64 * (h % 8):64 * (h % 8) + 64],
                                         lt_all[:, 128 * h:128 * h + 128],
                                         v_t[:, 1024 * rc + 64 * h:1024 * rc + 64 * h + 64],
                                         start=True, stop=True),
                                t.matmul(carry_ps[0:1, 64 * (h % 8):64 * (h % 8) + 64],
                                         lt_all[:, 128 * h + 127:128 * h + 128],
                                         v_t[:, 1024 * rc + 64 * h:1024 * rc + 64 * h + 64],
                                         start=True, stop=True))][-1],
                          reads=["v", "lt", f"pb{sbk}", "pbt"], writes=[f"pb{sbk}", "pbt"],
                          name="scan")
                        S("dve", lambda d, half=half, rc=rc, sbk=sbk: d.tensor_mul(
                            y_t[:, 1024 * rc + 512 * half:1024 * rc + 512 * half + 512],
                            q_t[:, 1024 * rc + 512 * half:1024 * rc + 512 * half + 512],
                            pb[sbk][:]),
                          reads=[f"pb{sbk}", "q"], writes=["y", f"pb{sbk}"])
                        if rc == RB - 1:
                            S("act", lambda a, half=half: a.activation(
                                carry_sb[0:1, 512 * half:512 * half + 512],
                                carry_ps[0:1, 0:512], AF.Copy),
                              reads=["pbt"], writes=["carrysb", "pbt"], small=True)

                if t_idx == 0:
                    DBG("y", y_t[:], "y")
                if l0_lvl < 7:
                    for rb in range(RB):
                        S("dve", lambda d, rb=rb: d.tensor_copy(
                            x1_t[:, 1024 * rb:1024 * rb + 1024],
                            x_t[:, 1024 * rb:1024 * rb + 1024]),
                          reads=["x"], writes=["x1"])
                if l0_lvl >= 6:
                    ln_stats(y_t, "y", 32, ST, STR, scr_f, "scr")
                for rb in range(RB if l0_lvl >= 6 else 0):
                    S("act", lambda a, rb=rb, ST=ST: a.activation(
                        xn_tmp[:, 1024 * rb:1024 * rb + 1024],
                        y_t[:, 1024 * rb:1024 * rb + 1024],
                        AF.Identity, scale=ST[:, 36 + rb:37 + rb],
                        bias=ST[:, 40 + rb:41 + rb]),
                      reads=["y", STR, "xn_tmp"], writes=["xn_tmp"], name="ln1ap")
                for rb in range(RB if l0_lvl >= 6 else 0):
                    S("dve", lambda d, rb=rb: d.tensor_mul(
                        scr_f[:, 1024 * rb:1024 * rb + 1024],
                        xn_tmp[:, 1024 * rb:1024 * rb + 1024], g0g_bc[:]),
                      reads=["xn_tmp", "g0g", "scr"], writes=["scr"])
                    S("dve", lambda d, rb=rb: d.tensor_add(
                        scr_f[:, 1024 * rb:1024 * rb + 1024],
                        scr_f[:, 1024 * rb:1024 * rb + 1024], b0b_bc[:]),
                      reads=["scr", "b0b"], writes=["scr"])
                    S("dve", lambda d, rb=rb: d.tensor_mul(
                        y_t[:, 1024 * rb:1024 * rb + 1024],
                        scr_f[:, 1024 * rb:1024 * rb + 1024],
                        g0_t[:, 1024 * rb:1024 * rb + 1024]),
                      reads=["scr", "g0"], writes=["y"])
                for hb in range(32 if l0_lvl >= 3 else 0):
                    buf = w2sl[hb % 2]
                    sp_dma([(buf[:], dcache["d_w2"].ap()[128 * hb:128 * hb + 128, :])],
                           reads=["d_w2", f"w2sl{hb % 2}"], writes=[f"w2sl{hb % 2}"],
                           name="w2sl")
                    S("pe", lambda t, buf=buf, hb=hb: [
                        t.matmul(bank_ap(2 * tb + fc),
                                 h_fm[:, 512 * hb + 128 * tb:512 * hb + 128 * tb + 128],
                                 buf[:, 512 * fc:512 * fc + 512],
                                 start=(hb == 0), stop=(hb == 31))
                        for tb in range(RB) for fc in range(2)][-1],
                      reads=[f"w2sl{hb % 2}", "h"],
                      writes=[bank_res(g) for g in range(8)], name="z_mm")
                for tb in range(RB if l0_lvl >= 3 else 0):
                    for fc in range(2):
                        g_ = 2 * tb + fc
                        S("dve", lambda d, tb=tb, fc=fc, g_=g_: d.tensor_add(
                            scr_f[:, 1024 * tb + 512 * fc:1024 * tb + 512 * fc + 512],
                            bank_ap(g_), b2_bc[:, 512 * fc:512 * fc + 512]),
                          reads=[bank_res(g_), "b2bc", "scr"], writes=["scr"])
                        S("act", lambda a, tb=tb, fc=fc: a.activation(
                            g_t[:, 1024 * tb + 512 * fc:1024 * tb + 512 * fc + 512],
                            scr_f[:, 1024 * tb + 512 * fc:1024 * tb + 512 * fc + 512],
                            AF.Silu),
                          reads=["scr"], writes=["g", bank_res(g_)])

                if t_idx == 0:
                    DBG("h", h_fm[:, 0:4096], "h")
                    DBG("g", g_t[:], "g")
                if t_idx == 0:
                    DBG("ygt", y_t[:], "y")
                transpose_tile(y_t, yg_fm, "y", "yg")

                for fc in range(2):
                    buf = stream_wt("d_wo", fc, tag="wo")
                    for tb in range(RB):
                        bk = tb % 2
                        S("pe", lambda t, buf=buf, tb=tb, bk=bk: [
                            t.matmul(pb[bk][:],
                                     yg_fm[:, 512 * eb + 128 * tb:512 * eb + 128 * tb + 128],
                                     buf[:, 512 * eb:512 * eb + 512],
                                     start=(eb == 0), stop=(eb == EB - 1))
                            for eb in range(EB)][-1],
                          reads=["yg", f"wbuf{fc % 2}"], writes=[f"pb{bk}"], name="wo_mm")
                        S("dve", lambda d, tb=tb, fc=fc, bk=bk: d.tensor_mul(
                            scr_f[:, 1024 * tb + 512 * fc:1024 * tb + 512 * fc + 512],
                            g_t[:, 1024 * tb + 512 * fc:1024 * tb + 512 * fc + 512],
                            pb[bk][:]),
                          reads=[f"pb{bk}", "g", "scr"], writes=["scr"])
                        S("dve", lambda d, tb=tb, fc=fc, bk=bk: d.tensor_add(
                            x1_t[:, 1024 * tb + 512 * fc:1024 * tb + 512 * fc + 512],
                            x_t[:, 1024 * tb + 512 * fc:1024 * tb + 512 * fc + 512],
                            scr_f[:, 1024 * tb + 512 * fc:1024 * tb + 512 * fc + 512]),
                          reads=["scr", "x"], writes=["x1", f"pb{bk}"])
            else:
                for rb in range(RB):
                    S("dve", lambda d, rb=rb: d.tensor_copy(
                        x1_t[:, 1024 * rb:1024 * rb + 1024],
                        x_t[:, 1024 * rb:1024 * rb + 1024]),
                      reads=["x"], writes=["x1"])

            if t_idx + 1 < n_tiles:
                nb, nch_ = divmod(t_idx + 1, NCH)
                nrow0 = nb * L + nch_ * 512
                sp_dma([(x_t[:].rearrange("p (a f) -> p a f", f=D),
                         x_in.ap()[nrow0:nrow0 + 512, :].rearrange("(a p) f -> p a f", p=128))],
                       reads=["x"], writes=["x"], name="xpref")

            if 1 not in layers and t_idx + 1 < n_tiles and 0 in layers and l0_lvl >= 1:
                l0_prolog(t_idx + 1)

            if 1 in layers:
                ln_stats(x1_t, "x1", 0, ST, STR, scr_f, "scr")
                for rb in range(RB):
                    S("act", lambda a, rb=rb, ST=ST: a.activation(
                        y_t[:, 1024 * rb:1024 * rb + 1024],
                        x1_t[:, 1024 * rb:1024 * rb + 1024],
                        AF.Identity, scale=ST[:, 4 + rb:5 + rb],
                        bias=ST[:, 8 + rb:9 + rb]),
                      reads=["x1", STR, "y"], writes=["y"], name="ln2ap")
                transpose_tile(y_t, xn1_fm, "y", "xn1_fm")

                for fc in range(4):
                    buf = stream_wt("d_wqkv", fc, tag="qk")
                    for fb in range(4):
                        fglob = 512 * fc + 128 * fb
                        bk = fb % 2
                        S("pe", lambda t, buf=buf, fb=fb, bk=bk: [
                            t.matmul(pb[bk][:],
                                     buf[:, 512 * eb + 128 * fb:512 * eb + 128 * fb + 128],
                                     xn1_fm[:, 512 * eb:512 * eb + 512],
                                     start=(eb == 0), stop=(eb == EB - 1))
                            for eb in range(EB)][-1],
                          reads=[f"wbuf{fc % 2}", "xn1_fm"], writes=[f"pb{bk}"],
                          name="qk_mm")
                        if fglob < 1024:
                            ebq = fglob // 128
                            S("act", lambda a, ebq=ebq, bk=bk: a.activation(
                                yg_fm[:, 512 * ebq:512 * ebq + 512], pb[bk][:],
                                AF.Identity, scale=0.125),
                              reads=[f"pb{bk}"], writes=["q1", f"pb{bk}"])
                        else:
                            ebk = (fglob - 1024) // 128
                            S("act", lambda a, ebk=ebk, bk=bk: a.activation(
                                v_t[:, 512 * ebk:512 * ebk + 512], pb[bk][:], AF.Copy),
                              reads=[f"pb{bk}"], writes=["k1", f"pb{bk}"])
                for fc in range(2):
                    buf = stream_wt("d_wqkv", 4 + fc, tag="v1")
                    for tb in range(RB):
                        bk = tb % 2
                        S("pe", lambda t, buf=buf, tb=tb, bk=bk: [
                            t.matmul(pb[bk][:],
                                     xn1_fm[:, 512 * eb + 128 * tb:512 * eb + 128 * tb + 128],
                                     buf[:, 512 * eb:512 * eb + 512],
                                     start=(eb == 0), stop=(eb == EB - 1))
                            for eb in range(EB)][-1],
                          reads=["xn1_fm", f"wbuf{fc % 2}"], writes=[f"pb{bk}"],
                          name="v1_mm")
                        S("act", lambda a, tb=tb, fc=fc, bk=bk: a.activation(
                            q_t[:, 1024 * tb + 512 * fc:1024 * tb + 512 * fc + 512],
                            pb[bk][:], AF.Copy),
                          reads=[f"pb{bk}"], writes=["v1", f"pb{bk}"])


                def _n(e, ins):
                    e.nop()
                    return ins
                S("dve", lambda d, ST=ST: _n(d, d.tensor_mul(ST[:, 48:52], ST[:, 20:24],
                                                             ST[:, 4:8])),
                  reads=[STR], writes=[STR], small=True)
                S("dve", lambda d, ST=ST: _n(d, d.tensor_mul(ST[:, 60:64], ST[:, 48:52],
                                                             ST[:, 4:8])),
                  reads=[STR], writes=[STR], small=True)
                S("act", lambda a, ST=ST: _n(a, a.activation(ST[:, 52:56], ST[:, 60:64],
                                                             AF.Sqrt, bias=EPS)),
                  reads=[STR], writes=[STR], small=True)
                S("dve", lambda d, ST=ST: _n(d, d.reciprocal(ST[:, 56:60], ST[:, 52:56])),
                  reads=[STR], writes=[STR], small=True)
                S("pe", lambda t, ST=ST: [
                    t.transpose(pb[1][0:1, 128 * rb:128 * rb + 128],
                                ST[:, 56 + rb:57 + rb], eye_f[:])
                    for rb in range(RB)][-1],
                  reads=[STR, "eye", "pb1"], writes=["pb1"], name="r3tr")
                S("act", lambda a: a.activation(rd_row[0][:], pb[1][0:1, 0:512], AF.Copy),
                  reads=["pb1"], writes=["rr0", "pb1"], name="r3ev", small=True)
                S("pe", lambda t: t.matmul(pb[1][:], ones_bf[:], rd_row[0][:],
                                           start=True, stop=True),
                  reads=["rr0", "pb1"], writes=["pb1"], name="r3bc")
                for eb in range(EB):
                    S("dve", lambda d, eb=eb: d.tensor_mul(
                        scr_f[:, 512 * eb:512 * eb + 512],
                        ln3_fm[:, 512 * eb:512 * eb + 512], pb[1][:]),
                      reads=["xn1_fm", "pb1", "scr"], writes=["scr"], name="l3mul")
                for eb in range(EB):
                    S("act", lambda a, eb=eb: a.activation(
                        ln3_fm[:, 512 * eb:512 * eb + 512],
                        scr_f[:, 512 * eb:512 * eb + 512],
                        AF.Identity, scale=g1_col[:, eb:eb + 1], bias=bt1_col[:, eb:eb + 1]),
                      reads=["scr", "g1c", "bt1c"], writes=["xn1_fm"], name="l3ap")

                for j2 in range(16):
                    buf = w1sl[j2 % 2]
                    sp_dma([(buf[:].rearrange("p (a f) -> p a f", f=256),
                             dcache["d_w1d"].ap()[:, 256 * j2:256 * j2 + 256]
                             .rearrange("(a p) f -> p a f", p=128))],
                           reads=["d_w1d", f"w1sl{j2 % 2}"], writes=[f"w1sl{j2 % 2}"],
                           name="w1dsl")
                    for k in range(2):
                        hb = 2 * j2 + k
                        bk = hb % 2
                        S("pe", lambda t, buf=buf, k=k, bk=bk: [
                            t.matmul(pb[bk][:],
                                     buf[:, 256 * eb + 128 * k:256 * eb + 128 * k + 128],
                                     ln3_fm[:, 512 * eb:512 * eb + 512],
                                     start=(eb == 0), stop=(eb == EB - 1))
                            for eb in range(EB)][-1],
                          reads=[f"w1sl{j2 % 2}", "xn1_fm"], writes=[f"pb{bk}"], name="u_mm")
                        S("act", lambda a, hb=hb, bk=bk: a.activation(
                            h_fm[:, 512 * hb:512 * hb + 512], pb[bk][:], AF.Silu,
                            bias=fc1b_col[:, hb:hb + 1]),
                          reads=[f"pb{bk}", "fc1c"], writes=["h", f"pb{bk}"], name="uev")
                for hb in range(32):
                    buf = w2sl[hb % 2]
                    sp_dma([(buf[:], dcache["d_w2d"].ap()[128 * hb:128 * hb + 128, :])],
                           reads=["d_w2d", f"w2sl{hb % 2}"], writes=[f"w2sl{hb % 2}"],
                           name="w2dsl")
                    S("pe", lambda t, buf=buf, hb=hb: [
                        t.matmul(bank_ap(2 * tb + fc),
                                 h_fm[:, 512 * hb + 128 * tb:512 * hb + 128 * tb + 128],
                                 buf[:, 512 * fc:512 * fc + 512],
                                 start=(hb == 0), stop=(hb == 31))
                        for tb in range(RB) for fc in range(2)][-1],
                      reads=[f"w2sl{hb % 2}", "h"],
                      writes=[bank_res(g) for g in range(8)], name="z2_mm")
                for tb in range(RB):
                    for fc in range(2):
                        g_ = 2 * tb + fc
                        S("dve", lambda d, tb=tb, fc=fc, g_=g_: d.scalar_tensor_tensor(
                            scr_f[:, 1024 * tb + 512 * fc:1024 * tb + 512 * fc + 512],
                            bank_ap(g_), 0.5, hfc2_bc[:, 512 * fc:512 * fc + 512],
                            ALU.mult, ALU.add),
                          reads=[bank_res(g_), "hfc2", "scr"], writes=["scr"])
                        S("dve", lambda d, tb=tb, fc=fc: d.tensor_add(
                            scr_f[:, 1024 * tb + 512 * fc:1024 * tb + 512 * fc + 512],
                            scr_f[:, 1024 * tb + 512 * fc:1024 * tb + 512 * fc + 512],
                            y_t[:, 1024 * tb + 512 * fc:1024 * tb + 512 * fc + 512]),
                          reads=["scr", "y"], writes=["scr"])
                        S("act", lambda a, tb=tb, fc=fc, g_=g_: a.activation(
                            g_t[:, 1024 * tb + 512 * fc:1024 * tb + 512 * fc + 512],
                            scr_f[:, 1024 * tb + 512 * fc:1024 * tb + 512 * fc + 512],
                            AF.Silu),
                          reads=["scr"], writes=["g", bank_res(g_)])

                if t_idx + 1 < n_tiles and 0 in layers and l0_lvl >= 1:
                    l0_prolog(t_idx + 1)

                def att_st(h):
                    ebh = h // 2
                    base = 64 * (h % 2)
                    par = h % 4
                    for kb in range(RB):
                        bk = (2 * (h % 2)) + (kb % 2)
                        S("pe", lambda t, h=h, kb=kb, bk=bk, ebh=ebh, base=base: t.matmul(
                            pb[bk][:],
                            v_t[base:base + 64,
                                512 * ebh + 128 * kb:512 * ebh + 128 * kb + 128],
                            yg_fm[base:base + 64, 512 * ebh:512 * ebh + 512],
                            start=True, stop=True),
                          reads=["k1", "q1", f"pb{bk}"], writes=[f"pb{bk}"], name="st_mm")
                        lam_col = 16 * (4 * ch + kb) + h
                        S("act", lambda a, kb=kb, par=par, bk=bk, lam_col=lam_col:
                          a.activation(
                              h_fm[:, 2048 * par + 512 * kb:2048 * par + 512 * kb + 512],
                              pb[bk][:], AF.Exp, scale=lamT[:, lam_col:lam_col + 1]),
                          reads=[f"pb{bk}", "lamT", f"at{par}"],
                          writes=[f"at{par}", f"pb{bk}"], name="expev")

                def att_dv(h):
                    par = h % 4
                    base = 64 * (h % 2)
                    S("pe", lambda t, h=h, par=par: [
                        t.matmul(pb[4][0:16, :], e16_sb[:, 16 * h:16 * h + 16],
                                 h_fm[:, 2048 * par + 512 * kb:2048 * par + 512 * kb + 512],
                                 start=(h == 0 and kb == 0),
                                 stop=(h == NH - 1 and kb == RB - 1))
                        for kb in range(RB)][-1],
                      reads=[f"at{par}", "e16", "pb4"], writes=["pb4"], name="dn_mm")
                    bk5 = 5 + (h % 4) // 2
                    S("pe", lambda t, h=h, par=par, base=base, bk5=bk5: [
                        t.matmul(pb[bk5][base:base + 64, :],
                                 q_t[:, 1024 * kb + 64 * h:1024 * kb + 64 * h + 64],
                                 h_fm[:, 2048 * par + 512 * kb:2048 * par + 512 * kb + 512],
                                 start=(kb == 0), stop=(kb == RB - 1),
                                 tile_position=(0, base) if base else None)
                        for kb in range(RB)][-1],
                      reads=[f"at{par}", "v1", f"pb{bk5}"], writes=[f"pb{bk5}"],
                      name="av_mm")
                    if h % 4 == 3:
                        ebp = h // 4
                        S("dve", lambda d, ebp=ebp: [
                            d.tensor_copy(
                                xn_fm[:, 512 * (2 * ebp + i):512 * (2 * ebp + i) + 512],
                                pb[5 + i][:]) for i in range(2)][-1],
                          reads=["pb5", "pb6"], writes=["att", "pb5", "pb6"],
                          name="attev")

                att_st(0)
                att_st(1)
                for h in range(NH):
                    if h + 2 < NH:
                        att_st(h + 2)
                    att_dv(h)
                S("dve", lambda d: d.reciprocal(scr_f[0:16, 0:512], pb[4][0:16, :]),
                  reads=["pb4", "scr"], writes=["scr", "pb4"])
                S("act", lambda a: a.activation(dn_sb[:], scr_f[0:16, 0:512], AF.Copy),
                  reads=["scr"], writes=["dn"], small=True)
                pl_dma([(dcache["d_dn"].ap(), dn_sb[:])], reads=["dn"], writes=["d_dn"])
                for h in range(NH):
                    rr = rd_row[h % 2]
                    pl_dma([(rr[:], dcache["d_dn"].ap()[h:h + 1, :])],
                           reads=["d_dn", f"rr{h % 2}"], writes=[f"rr{h % 2}"])
                    base = 64 * (h % 2)
                    S("pe", lambda t, rr=rr, base=base: t.matmul(
                        pb[4][base:base + 64, :], ones_bf[0:1, 0:64], rr[:],
                        start=True, stop=True,
                        tile_position=(0, base) if base else None),
                      reads=[f"rr{h % 2}", "pb4"], writes=["pb4"], name="rd_mm")
                    if h % 2 == 1:
                        ebh2 = h // 2
                        S("dve", lambda d, ebh2=ebh2: d.tensor_mul(
                            xn_fm[:, 512 * ebh2:512 * ebh2 + 512],
                            xn_fm[:, 512 * ebh2:512 * ebh2 + 512], pb[4][:]),
                          reads=["att", "pb4"], writes=["att", "pb4"])
                for fc in range(2):
                    buf = stream_wt("d_wout", fc, tag="wop")
                    for tb in range(RB):
                        bk = tb % 2
                        S("pe", lambda t, buf=buf, tb=tb, bk=bk: [
                            t.matmul(pb[bk][:],
                                     xn_fm[:, 512 * eb + 128 * tb:512 * eb + 128 * tb + 128],
                                     buf[:, 512 * eb:512 * eb + 512],
                                     start=(eb == 0), stop=(eb == EB - 1))
                            for eb in range(EB)][-1],
                          reads=["att", f"wbuf{fc % 2}"], writes=[f"pb{bk}"],
                          name="wout_mm")
                        S("dve", lambda d, tb=tb, fc=fc, bk=bk: d.tensor_add(
                            scr_f[:, 1024 * tb + 512 * fc:1024 * tb + 512 * fc + 512],
                            pb[bk][:], bout_bc[:, 512 * fc:512 * fc + 512]),
                          reads=[f"pb{bk}", "boutb", "scr"], writes=["scr"])
                        S("dve", lambda d, tb=tb, fc=fc: d.tensor_mul(
                            scr_f[:, 1024 * tb + 512 * fc:1024 * tb + 512 * fc + 512],
                            scr_f[:, 1024 * tb + 512 * fc:1024 * tb + 512 * fc + 512],
                            g_t[:, 1024 * tb + 512 * fc:1024 * tb + 512 * fc + 512]),
                          reads=["scr", "g"], writes=["scr"])
                        S("dve", lambda d, tb=tb, fc=fc, bk=bk: d.tensor_add(
                            x1_t[:, 1024 * tb + 512 * fc:1024 * tb + 512 * fc + 512],
                            x1_t[:, 1024 * tb + 512 * fc:1024 * tb + 512 * fc + 512],
                            scr_f[:, 1024 * tb + 512 * fc:1024 * tb + 512 * fc + 512]),
                          reads=["scr", "x1"], writes=["x1", f"pb{bk}"])

            pl_dma([(x_out.ap()[row0:row0 + 512, :].rearrange("(a p) f -> p a f", p=128),
                     x1_t[:].rearrange("p (a f) -> p a f", f=D))],
                   reads=["x1"], writes=["xoutd"], name="out")

    for q, host in (("sp", "sp"), ("pl", "pl"), ("aq", "act")):
        fins = P.queue_fin(q)

        def mkfin(fins):
            def f(obj):
                for sem, val in fins:
                    obj.wait_ge(sem, val)
            return f
        P.steps[host].append(("q", mkfin(fins), {}, "fin", None))

    with nc.Block() as blk:
        @blk.sync
        def _(sp):
            P.emit_engine("sp", sp)

        @blk.gpsimd
        def _(g):
            P.emit_engine("pl", g)

        @blk.tensor
        def _(t):
            P.emit_engine("pe", t)

        @blk.scalar
        def _(a):
            P.emit_engine("act", a)

        @blk.vector
        def _(d):
            P.emit_engine("dve", d)

    return nc


_NC_CACHE = {}


def kernel(**inputs):
    layers = (0, 1)
    n_batch = 2
    L = 4096
    key = (layers, n_batch, L)
    if key not in _NC_CACHE:
        _NC_CACHE[key] = build_program(layers, n_batch, L)
    nc = _NC_CACHE[key]

    x = np.ascontiguousarray(inputs["L0_attn_x"])
    in_maps = []
    for c in range(N_CORES):
        m = {"x_in": x[n_batch * c:n_batch * (c + 1)].reshape(n_batch * L, D)}
        for k, v in inputs.items():
            if k != "L0_attn_x":
                m[k] = np.asarray(v)
        in_maps.append(m)
    res = run_bass_kernel_spmd(nc, in_maps, core_ids=list(range(N_CORES)))
    out = np.concatenate([r["x_out"].reshape(n_batch, L, D) for r in res.results],
                         axis=0)
    return out.astype(np.float32, copy=False)
